# version 06 (final)
# baseline (speedup 1.0000x reference)
import math
from contextlib import ExitStack

import numpy as np
import ml_dtypes

import concourse.bass as bass
import concourse.mybir as mybir
from concourse.bass_utils import run_bass_kernel_spmd

F32 = mybir.dt.float32
BF16 = mybir.dt.bfloat16
AF = mybir.ActivationFunctionType
ALU = mybir.AluOpType
AX = mybir.AxisListType

BF = ml_dtypes.bfloat16

D = 1024
NCHUNK = 8

ENG_SECTIONS = {
    "pe": "tensor",
    "act": "scalar",
    "dve": "vector",
    "pool": "gpsimd",
    "sp": "sync",
}


class Prog:

    def __init__(self, nc):
        self.nc = nc
        self.ops = {e: [] for e in ENG_SECTIONS}
        self.cnt = {}
        self.sem_names = []

    def _sem(self, name):
        if name not in self.cnt:
            self.cnt[name] = 0
            self.sem_names.append(name)
        return name

    def op(self, eng, fn, waits=(), inc=None, amt=1):
        ws = [(s, v) for (s, v) in waits if v > 0]
        if inc is not None:
            self._sem(inc)
            self.cnt[inc] += amt
        self.ops[eng].append((fn, ws, inc, amt))
        if inc is not None:
            return (inc, self.cnt[inc])
        return None

    def dma(self, queue, out, in_, waits=()):
        sem = "d_" + queue
        return self.op(
            queue,
            lambda e, o=out, i=in_: e.dma_start(out=o, in_=i),
            waits=waits,
            inc=sem,
            amt=16,
        )

    def now(self, sem):
        return (sem, self.cnt.get(sem, 0))

    def all_now(self):
        return [(s, v) for s, v in self.cnt.items() if v > 0]

    def build(self, es):
        handles = {s: es.enter_context(self.nc.semaphore(s)) for s in self.sem_names}
        block = es.enter_context(self.nc.Block())
        for eng, section in ENG_SECTIONS.items():
            ops = self.ops[eng]

            def body(e, ops=ops):
                waited = {}
                for fn, ws, inc, amt in ops:
                    for s, v in ws:
                        if waited.get(s, 0) < v:
                            e.wait_ge(handles[s], v)
                            waited[s] = v
                    inst = fn(e)
                    if inc is not None:
                        assert inst is not None
                        inst.then_inc(handles[inc], amt)

            getattr(block, section)(body)



def fm_slab(x_bt_d):
    T = x_bt_d.shape[0]
    return np.ascontiguousarray(
        x_bt_d.T.reshape(NCHUNK, 128, T).astype(np.float32)
    )


def from_fm_slab(slab):
    return np.ascontiguousarray(slab.reshape(D, -1).T)


def w_slab(w_in_out):
    din, dout = w_in_out.shape
    return np.ascontiguousarray(w_in_out.reshape(din // 128, 128, dout).astype(np.float32))


def col(v):
    return np.ascontiguousarray(np.asarray(v, np.float32).reshape(-1, 1))



class Consts:

    def __init__(self, nc, es, p, host, need_identf=False):
        self.p = p
        self.ones_col = es.enter_context(nc.sbuf_tensor("c_ones_col", [128, 1], BF16))
        self.ones_row = es.enter_context(nc.sbuf_tensor("c_ones_row", [1, 128], BF16))
        self.ident = es.enter_context(nc.sbuf_tensor("c_ident", [128, 128], BF16))
        self.fcol = es.enter_context(nc.sbuf_tensor("c_fcol", [128, 16], F32))
        dr_id = nc.dram_tensor("c_ident_d", [128, 128], F32, kind="ExternalInput").ap()
        host["c_ident_d"] = np.eye(128, dtype=np.float32)
        p.dma("pool", self.ident[:, :], dr_id)
        if need_identf:
            self.identf = es.enter_context(
                nc.sbuf_tensor("c_identf", [128, 128], F32))
            p.dma("sp", self.identf[:, :], dr_id)
        p.op("pool", lambda e: e.memset(self.ones_col[:, :], 1.0), inc="c0")
        p.op("pool", lambda e: e.memset(self.ones_row[:, :], 1.0), inc="c0")
        self.map = {}

    def __getitem__(self, k):
        return getattr(self, k)

    def bias(self, val, parts=128):
        val = float(val)
        if val not in self.map:
            j = len(self.map)
            assert j < 16
            tok = self.p.op(
                "pool",
                lambda e, j=j, val=val: e.memset(self.fcol[:, j:j + 1], val),
                inc="c0")
            self.map[val] = (j, tok)
        j, tok = self.map[val]
        return self.fcol[0:parts, j:j + 1], tok


def alloc_consts(nc, es, p, host, need_identf=False):
    return Consts(nc, es, p, host, need_identf)


def ln_rows(p, sbuf, name, T):
    return sbuf(name, [1, T], F32)



def build_l0_attn(nc, es, p, c, xin_d, xout_d, wd, S, dbg=None):
    NB = 2
    H = 16
    HD = 64
    NG = 4
    NT = S // 512
    NKT = S // 128

    def sbuf(name, shape, dtype):
        return es.enter_context(nc.sbuf_tensor("sb_" + name, shape, dtype))

    def psum(name, shape):
        return es.enter_context(nc.psum_tensor("ps_" + name, shape, F32))

    xin = sbuf("xin", [128, NCHUNK, S], BF16)
    qg = sbuf("qg", [128, 2, S], BF16)
    kg = sbuf("kg", [128, 2, S], BF16)
    vgt = sbuf("vgt", [128, NKT, 256], BF16)
    Pst = [sbuf("pst%d" % i, [128, 512], BF16) for i in range(2)]
    ctx = sbuf("ctx", [128, NCHUNK, S], BF16)
    ybuf = sbuf("ybuf", [128, NCHUNK, S], BF16)
    wq = sbuf("wq", [128, NCHUNK, D], BF16)
    wk = sbuf("wk", [128, NCHUNK, D], BF16)
    wv = sbuf("wv", [128, NCHUNK, D], BF16)
    wo = sbuf("wo", [128, NCHUNK, D], BF16)
    bcol = sbuf("bcol", [128, NCHUNK, 3], F32)
    nr = sbuf("nr", [1, 512], BF16)
    srow = sbuf("srow", [1, 3 * 512], F32)
    mrow = sbuf("mrow", [1, 2 * NT * 512], BF16)
    sqt = sbuf("sqt", [128, 512], BF16)
    x1t = sbuf("x1t", [128, 2, 512], F32)

    pS = [psum("pS%d" % i, [128, 512]) for i in range(2)]
    pN = psum("pN", [1, 512])
    pCtx = psum("pCtx", [64, 512])
    pBC = [psum("pBC%d" % i, [128, 512]) for i in range(2)]
    pPrj = psum("pPrj", [128, 512])
    pSt = psum("pSt", [1, 512])

    for cc in range(NCHUNK):
        p.dma("pool", wq[:, cc, :], wd["a_wq"][cc, :, :])
        p.dma("pool", wk[:, cc, :], wd["a_wk"][cc, :, :])
        p.dma("pool", wv[:, cc, :], wd["a_wv"][cc, :, :])
        p.dma("pool", wo[:, cc, :], wd["a_wo"][cc, :, :])
        p.dma("sp", bcol[:, cc, :], wd["a_bcol"][cc, :, :])
    wload = [p.now("d_pool"), p.now("d_sp"), p.now("c0")]

    for b in range(NB):
        t0 = b * S
        xw = [p.now("pe"), p.now("dve")] if b > 0 else []
        for cc in range(NCHUNK):
            p.dma("pool", xin[:, cc, :], xin_d[cc, :, t0:t0 + S], waits=xw)
        xload = p.now("d_pool")

        for g in range(NG):
            waitw = wload + [xload]
            for tgt, wt, dst in (("q", wq, qg), ("k", wk, kg)):
                for co in range(2):
                    for tt in range(NT):
                        def mmgroup(e, wt=wt, co=co, tt=tt, g=g):
                            for cin in range(NCHUNK):
                                i = e.matmul(
                                    pPrj[:, :],
                                    wt[:, cin, g * 256 + co * 128:g * 256 + co * 128 + 128],
                                    xin[:, cin, tt * 512:(tt + 1) * 512],
                                    start=(cin == 0), stop=(cin == NCHUNK - 1))
                            return i
                        mtok = p.op("pe", mmgroup, waits=waitw + [p.now("dve")], inc="pe")
                        p.op("dve",
                             lambda e, dst=dst, co=co, tt=tt: e.tensor_copy(
                                 dst[:, co, tt * 512:(tt + 1) * 512], pPrj[:, :]),
                             waits=[mtok], inc="dve")
            for ti in range(NKT):
                def mmv(e, ti=ti, g=g):
                    for cin in range(NCHUNK):
                        i = e.matmul(
                            pS[0][:, 0:256],
                            xin[:, cin, ti * 128:(ti + 1) * 128],
                            wv[:, cin, g * 256:(g + 1) * 256],
                            start=(cin == 0), stop=(cin == NCHUNK - 1))
                    return i
                mtok = p.op("pe", mmv, waits=waitw + [p.now("dve")], inc="pe")
                p.op("dve",
                     lambda e, ti=ti: e.tensor_copy(vgt[:, ti, :], pS[0][:, 0:256]),
                     waits=[mtok], inc="dve")
            projdone = [p.now("pe"), p.now("dve")]
            if dbg is not None and b == 0 and g == 0:
                p.dma("pool", dbg["qg"], qg[:, :, :], waits=projdone)
                p.dma("pool", dbg["kg"], kg[:, :, :], waits=projdone)
                p.dma("pool", dbg["vgt"], vgt[:, :, :], waits=projdone)

            for hl in range(4):
                h = 4 * g + hl
                rb = 64 * (hl % 2)
                rc = hl // 2
                for qt in range(NT):
                    evac_toks = [None, None]
                    cons_toks = [None, None]
                    dve_guard = p.now("dve")
                    for kt in range(NKT):
                        pb = kt % 2
                        w = list(projdone) if kt < 2 else []
                        if kt == 0:
                            w = w + [dve_guard]
                        if cons_toks[pb] is not None:
                            w = w + [cons_toks[pb]]
                        stok = p.op(
                            "pe",
                            lambda e, rb=rb, rc=rc, kt=kt, qt=qt, pb=pb: e.matmul(
                                pS[pb][:, :],
                                kg[rb:rb + 64, rc, kt * 128:(kt + 1) * 128],
                                qg[rb:rb + 64, rc, qt * 512:(qt + 1) * 512],
                                start=True, stop=True),
                            waits=w, inc="pe")
                        etok = p.op(
                            "act",
                            lambda e, pb=pb: e.activation(
                                Pst[pb][:, :], pS[pb][:, :], AF.Relu, scale=0.125),
                            waits=[stok], inc="act")
                        evac_toks[pb] = etok
                        def mmnorm(e, pb=pb, kt=kt, hl=hl, last=(kt == NKT - 1)):
                            e.matmul(pN[:, :], c["ones_col"][:, :], Pst[pb][:, :],
                                     start=(kt == 0), stop=last)
                            return e.matmul(pCtx[:, :], vgt[:, kt, hl * 64:hl * 64 + 64],
                                            Pst[pb][:, :], start=(kt == 0), stop=last)
                        ntok = p.op("pe", mmnorm, waits=[etok], inc="pe")
                        cons_toks[pb] = ntok
                    def fin_norm(e):
                        e.tensor_scalar(srow[:, 0:512], pN[:, :], 1e-6, None, ALU.add)
                        return e.reciprocal(srow[:, 512:1024], srow[:, 0:512])
                    ftok = p.op("dve", fin_norm, waits=[cons_toks[0], cons_toks[1]], inc="dve")
                    ftok = p.op("act", lambda e: e.copy(nr[:, :], srow[:, 512:1024]),
                                waits=[ftok], inc="act")
                    btok = p.op(
                        "pe",
                        lambda e: e.matmul(pBC[0][:, :], c["ones_row"][:, :], nr[:, :],
                                           start=True, stop=True),
                        waits=[ftok], inc="pe")
                    def ctx_evac(e, h=h, qt=qt):
                        b0 = 64 * (h % 2)
                        dst = ctx[b0:b0 + 64, h // 2, qt * 512:(qt + 1) * 512]
                        e.tensor_copy(dst, pCtx[:, :])
                        return e.tensor_mul(dst, dst, pBC[0][b0:b0 + 64, :])
                    p.op("dve", ctx_evac, waits=[btok], inc="dve")
        attdone = [p.now("pe"), p.now("dve"), p.now("act")]
        if dbg is not None and b == 0:
            p.dma("pool", dbg["ctx"], ctx[:, :, :], waits=attdone)

        for qt in range(NT):
            for co in range(NCHUNK):
                def mmo(e, qt=qt, co=co):
                    for cin in range(NCHUNK):
                        i = e.matmul(pPrj[:, :],
                                     wo[:, cin, co * 128:co * 128 + 128],
                                     ctx[:, cin, qt * 512:(qt + 1) * 512],
                                     start=(cin == 0), stop=(cin == NCHUNK - 1))
                    return i
                mtok = p.op("pe", mmo, waits=attdone + [p.now("dve")], inc="pe")
                p.op("dve",
                     lambda e, qt=qt, co=co: e.scalar_tensor_tensor(
                         ybuf[:, co, qt * 512:(qt + 1) * 512],
                         xin[:, co, qt * 512:(qt + 1) * 512],
                         bcol[:, co, 0:1], pPrj[:, :], ALU.add, ALU.add),
                     waits=[mtok], inc="dve")
        if dbg is not None and b == 0:
            p.dma("pool", dbg["ybuf"],
                  ybuf[:, :, :], waits=[p.now("pe"), p.now("dve")])
        for stage in (0, 1):
            if dbg is not None and b == 0 and stage == 1:
                p.dma("pool", dbg["z"], ybuf[:, :, :],
                      waits=[p.now("pe"), p.now("dve")])
            prev = [p.now("pe"), p.now("dve"), p.now("act"), p.now("d_pool")]
            eps = 1e-5
            stok = None
            for qt in range(NT):
                for co in range(NCHUNK):
                    sqtok = p.op(
                        "act",
                        lambda e, qt=qt, co=co: e.activation(
                            sqt[:, :], ybuf[:, co, qt * 512:(qt + 1) * 512], AF.Square),
                        waits=prev + ([stok] if stok else []), inc="act")

                    def mmstat(e, qt=qt, co=co):
                        e.matmul(pN[:, :], c["ones_col"][:, :],
                                 ybuf[:, co, qt * 512:(qt + 1) * 512],
                                 start=(co == 0), stop=(co == NCHUNK - 1))
                        return e.matmul(pSt[:, :], c["ones_col"][:, :], sqt[:, :],
                                        start=(co == 0), stop=(co == NCHUNK - 1))
                    stok = p.op("pe", mmstat, waits=[sqtok], inc="pe")
                sl = slice(qt * 512, (qt + 1) * 512)

                def rowmath(e):
                    e.tensor_scalar(srow[:, 0:512], pN[:, :], 1.0 / D, None, ALU.mult)
                    e.tensor_scalar(srow[:, 512:1024], pSt[:, :], 1.0 / D, None, ALU.mult)
                    e.tensor_mul(srow[:, 1024:1536], srow[:, 0:512], srow[:, 0:512])
                    return e.tensor_sub(srow[:, 512:1024], srow[:, 512:1024],
                                        srow[:, 1024:1536])
                rtok = p.op("dve", rowmath, waits=[stok], inc="dve")
                bap, btk = c.bias(eps, 1)
                ltok = p.op(
                    "act",
                    lambda e, bap=bap: e.activation(srow[:, 1024:1536],
                                                    srow[:, 512:1024],
                                                    AF.Ln, bias=bap),
                    waits=[rtok, btk], inc="act")

                def rowfin(e, sl=sl):
                    e.mul(mrow[:, sl], srow[:, 0:512], -1.0)
                    return e.activation(mrow[:, NT * 512 + sl.start:NT * 512 + sl.stop],
                                        srow[:, 1024:1536], AF.Exp, scale=-0.5)
                ftok = p.op("act", rowfin, waits=[ltok], inc="act")
                stok = rtok
            statdone = [p.now("act"), p.now("dve"), p.now("pe")]
            if dbg is not None and b == 0:
                p.dma("pool", dbg["mrow%d" % stage], mrow[:, :], waits=statdone)
            dmatok = [p.now("d_sp"), p.now("d_sp")]
            for qt in range(NT):
                sl = slice(qt * 512, (qt + 1) * 512)

                def bcast(e, sl=sl):
                    e.matmul(pBC[0][:, :], c["ones_row"][:, :], mrow[0:1, sl],
                             start=True, stop=True)
                    return e.matmul(pBC[1][:, :], c["ones_row"][:, :],
                                    mrow[0:1, NT * 512 + sl.start:NT * 512 + sl.stop],
                                    start=True, stop=True)
                btok = p.op("pe", bcast, waits=statdone + [p.now("dve")], inc="pe")
                for co in range(NCHUNK):
                    if stage == 0:
                        def apply0(e, qt=qt, co=co):
                            ysl = ybuf[:, co, qt * 512:(qt + 1) * 512]
                            e.tensor_add(x1t[:, 0, :], ysl, pBC[0][:, :])
                            e.tensor_mul(x1t[:, 0, :], x1t[:, 0, :], pBC[1][:, :])
                            e.tensor_scalar(x1t[:, 0, :], x1t[:, 0, :],
                                            bcol[:, co, 1:2], bcol[:, co, 2:3],
                                            ALU.mult, ALU.add)
                            return e.tensor_add(ysl, x1t[:, 0, :],
                                                xin[:, co, qt * 512:(qt + 1) * 512])
                        p.op("dve", apply0, waits=[btok], inc="dve")
                    else:
                        xb = co % 2
                        w = [btok] + ([dmatok[xb]] if dmatok[xb] else [])

                        def apply1(e, qt=qt, co=co, xb=xb):
                            ysl = ybuf[:, co, qt * 512:(qt + 1) * 512]
                            e.tensor_add(x1t[:, xb, :], ysl, pBC[0][:, :])
                            return e.tensor_mul(x1t[:, xb, :], x1t[:, xb, :],
                                                pBC[1][:, :])
                        atok = p.op("dve", apply1, waits=w, inc="dve")
                        dmatok[xb] = p.dma(
                            "sp", xout_d[co, :, t0 + qt * 512:t0 + (qt + 1) * 512],
                            x1t[:, xb, :], waits=[atok])
    return



FFN_GRAPH = [
    (0, 0, 0, -1, -1, -1),
    (2, 1, 2, 0, -1, 0),
    (5, 3, 4, 1, -1, 0),
    (9, 6, 8, 2, -1, 1),
    (14, 10, 12, 3, -1, 1),
    (19, 15, 17, 4, 0, 2),
]


def build_l0_ffn(nc, es, p, c, xin_d, xout_d, node_d, wd, S, dbg=None):
    trow_d = nc.dram_tensor("ffn_trow", [16, 128], BF16, kind="Internal").ap()
    NB = 2
    H = 8
    NT = S // 512
    NKT = S // 128

    def sbuf(name, shape, dtype):
        return es.enter_context(nc.sbuf_tensor("fb_" + name, shape, dtype))

    def psum(name, shape, dtype=F32):
        return es.enter_context(nc.psum_tensor("fp_" + name, shape, dtype))

    qe = sbuf("qe", [128, NCHUNK, S], BF16)
    bB = sbuf("bB", [128, NCHUNK, S], BF16)
    qh = sbuf("qh", [128, NCHUNK, S], BF16)
    kh = sbuf("kh", [128, NCHUNK, S], BF16)
    vh = sbuf("vh", [128, NKT, D], BF16)
    wsb = sbuf("wsb", [128, NCHUNK, 512], BF16)
    st = sbuf("st", [128, 2, NCHUNK, 512], BF16)
    Pst = [sbuf("pst%d" % i, [128, 512], BF16) for i in range(2)]
    maxf = sbuf("maxf", [128, NKT], F32)
    nmaxb = sbuf("nmaxb", [128, NKT], BF16)
    nmrow = sbuf("nmrow", [16, 128], BF16)
    negrow = sbuf("negrow", [1, S], BF16)
    nr = sbuf("nr", [1, 512], BF16)
    srow = sbuf("srow", [1, 3 * 512], F32)
    mrow = negrow
    sqt = sbuf("sqt", [128, 512], BF16)
    x1t = sbuf("x1t", [128, 2, 512], F32)
    ntile = sbuf("ntile", [128, 2, 512], BF16)
    becol = sbuf("becol", [128, 20, NCHUNK], BF16)
    lncol = sbuf("lncol", [128, NCHUNK, 2], F32)
    scal = sbuf("scal", [128, 6, 3], F32)
    ewbc = sbuf("ewbc", [128, 24], F32)

    pb2 = psum("pb2", [128, 1024])
    pA = psum("pA", [128, 512])
    pB = psum("pB", [128, 512])
    pC = psum("pC", [128, 512])
    pD = psum("pD", [1, 512])
    pT = psum("pT", [16, 128], BF16)
    pX = psum("pX", [128, 512])

    p.dma("pool", becol[:, :, :], wd["f_becol"])
    p.dma("sp", lncol[:, :, :], wd["f_lncol"])
    p.dma("sp", ewbc[:, :], wd["f_ewact_bc"])
    lw = [p.now("d_sp")]

    def mkscal(e):
        for cn in range(6):
            e.scalar_tensor_tensor(scal[:, cn, 0:1], ewbc[:, 3 * cn:3 * cn + 1],
                                   1.0 / math.sqrt(128.0),
                                   ewbc[:, 3 * cn + 1:3 * cn + 2],
                                   ALU.mult, ALU.mult)
            e.tensor_mul(scal[:, cn, 1:2], ewbc[:, 18 + cn:19 + cn],
                         ewbc[:, 3 * cn + 2:3 * cn + 3])
            i = e.tensor_copy(scal[:, cn, 2:3], ewbc[:, 3 * cn:3 * cn + 1])
        return i
    sctok = p.op("dve", mkscal, waits=lw, inc="dve")

    def load_wslab(name, idx, half):
        guard = [p.now("pe")]
        for cc in range(NCHUNK):
            p.dma("pool", wsb[:, cc, :],
                  wd[name][idx, cc, :, half * 512:(half + 1) * 512], waits=guard)
        return p.now("d_pool")

    def src_tile_load(src_d, b, tt, sbf, f32src, waits=()):
        q = "pool" if f32src else "sp"
        return p.dma(q, st[:, sbf, :, :],
                     src_d[:, :, b * S + tt * 512:b * S + (tt + 1) * 512]
                     .rearrange("c p t -> p c t"), waits=waits)

    def proj_fm(wname, widx, src_d, b, dst, bias=None, f32src=False,
                src_done=None):
        stoks = [None, None]
        for half in range(2):
            wtok = load_wslab(wname, widx, half)
            for tt in range(NT):
                sb = tt % 2
                w = [wtok] + ([stoks[sb]] if stoks[sb] else [])
                sw = [stoks[sb] or p.now("pe")]
                if src_done is not None:
                    sw.append(src_done)
                ldt = src_tile_load(src_d, b, tt, sb, f32src, waits=sw)
                for cl in range(4):
                    co = half * 4 + cl
                    def mmg(e, cl=cl, sb=sb):
                        for cin in range(NCHUNK):
                            i = e.matmul(pX[:, :],
                                         wsb[:, cin, cl * 128:cl * 128 + 128],
                                         st[:, sb, cin, :],
                                         start=(cin == 0), stop=(cin == NCHUNK - 1))
                        return i
                    mtok = p.op("pe", mmg,
                                waits=w + [ldt, p.now("act"), p.now("dve")],
                                inc="pe")
                    if bias is not None:
                        p.op("act",
                             lambda e, tt=tt, co=co, bias=bias: e.activation(
                                 dst[:, co, tt * 512:(tt + 1) * 512], pX[:, :],
                                 AF.Identity, bias=bias[:, co:co + 1]),
                             waits=[mtok], inc="act")
                    else:
                        p.op("dve",
                             lambda e, tt=tt, co=co: e.tensor_copy(
                                 dst[:, co, tt * 512:(tt + 1) * 512], pX[:, :]),
                             waits=[mtok], inc="dve")
                stoks[sb] = p.now("pe")
        return [p.now("pe"), p.now("act"), p.now("dve")]

    def proj_fm_sbuf(wname, widx, src, dst):
        for half in range(2):
            wtok = load_wslab(wname, widx, half)
            for tt in range(NT):
                for cl in range(4):
                    co = half * 4 + cl
                    def mmg(e, tt=tt, cl=cl):
                        for cin in range(NCHUNK):
                            i = e.matmul(pX[:, :],
                                         wsb[:, cin, cl * 128:cl * 128 + 128],
                                         src[:, cin, tt * 512:(tt + 1) * 512],
                                         start=(cin == 0), stop=(cin == NCHUNK - 1))
                        return i
                    mtok = p.op("pe", mmg,
                                waits=[wtok, p.now("act"), p.now("dve")], inc="pe")
                    p.op("dve",
                         lambda e, tt=tt, co=co: e.tensor_copy(
                             dst[:, co, tt * 512:(tt + 1) * 512], pX[:, :]),
                         waits=[mtok], inc="dve")
        return [p.now("pe"), p.now("dve")]

    def proj_tm_sbuf(wname, widx, src, dst):
        for half in range(2):
            wtok = load_wslab(wname, widx, half)
            for ti in range(NKT):
                def mmg(e, ti=ti):
                    for cin in range(NCHUNK):
                        i = e.matmul(pX[:, :],
                                     src[:, cin, ti * 128:(ti + 1) * 128],
                                     wsb[:, cin, :],
                                     start=(cin == 0), stop=(cin == NCHUNK - 1))
                    return i
                mtok = p.op("pe", mmg,
                            waits=[wtok, p.now("act"), p.now("dve")], inc="pe")
                p.op("dve",
                     lambda e, ti=ti, half=half: e.tensor_copy(
                         dst[:, ti, half * 512:(half + 1) * 512], pX[:, :]),
                     waits=[mtok], inc="dve")
        return [p.now("pe"), p.now("dve")]

    srcs = [xin_d] + node_d
    node_done = {-1: None}

    for b in range(NB):
        for cn in range(6):
            eq, ek, ev, in_q, in_k, in_v = FFN_GRAPH[cn]
            shareQV = (eq == ev) and (in_q == in_v)
            shareQKV = shareQV and (eq == ek) and (in_q == in_k)
            proj_fm("f_We", eq, srcs[in_q + 1], b, qe,
                    bias=becol[:, eq, :], f32src=(in_q == -1),
                    src_done=node_done.get(in_q))
            if not shareQKV:
                proj_fm("f_We", ek, srcs[in_k + 1], b, bB,
                        bias=becol[:, ek, :], f32src=(in_k == -1),
                        src_done=node_done.get(in_k))
            proj_fm_sbuf("f_Wk", cn, qe if shareQKV else bB, kh)
            if not (shareQKV or shareQV):
                proj_fm("f_We", ev, srcs[in_v + 1], b, bB,
                        bias=becol[:, ev, :], f32src=(in_v == -1),
                        src_done=node_done.get(in_v))
            proj_tm_sbuf("f_Wv", cn, qe if (shareQKV or shareQV) else bB, vh)
            proj_fm_sbuf("f_Wq", cn, qe, qh)
            pdone = [p.now("pe"), p.now("act"), p.now("dve")]
            if dbg is not None and b == 0 and cn == 0:
                p.dma("pool", dbg["qe"], qe[:, :, :], waits=pdone)
                p.dma("pool", dbg["qh"], qh[:, :, :], waits=pdone)
                p.dma("pool", dbg["kh"], kh[:, :, :], waits=pdone)
                p.dma("pool", dbg["vh"], vh[:, :, :], waits=pdone)

            csap = scal[:, cn, 0:1]
            NQT = S // 128
            for h in range(H):
                guard = pdone
                for qt in range(NQT):
                    first = True
                    for kg in range((NT + 1) // 2):
                        nj = min(2, NT - 2 * kg)

                        def mm1(e, h=h, qt=qt, kg=kg, nj=nj):
                            for j in range(nj):
                                i = e.matmul(
                                    pb2[:, j * 512:(j + 1) * 512],
                                    qh[:, h, qt * 128:(qt + 1) * 128],
                                    kh[:, h, (kg * 2 + j) * 512:(kg * 2 + j + 1) * 512],
                                    start=True, stop=True)
                            return i
                        mtok = p.op("pe", mm1, waits=guard + [p.now("dve")],
                                    inc="pe")
                        if first:
                            rtok = p.op(
                                "dve",
                                lambda e, qt=qt, nj=nj: e.reduce_max(
                                    maxf[:, qt:qt + 1], pb2[:, 0:nj * 512],
                                    axis=AX.X),
                                waits=[mtok], inc="dve")
                            first = False
                        else:
                            rtok = p.op(
                                "dve",
                                lambda e, nj=nj: e.reduce_max(
                                    x1t[:, 0, 0:1], pb2[:, 0:nj * 512], axis=AX.X),
                                waits=[mtok], inc="dve")
                            rtok = p.op(
                                "dve",
                                lambda e, qt=qt: e.tensor_max(
                                    maxf[:, qt:qt + 1], maxf[:, qt:qt + 1],
                                    x1t[:, 0, 0:1]),
                                waits=[rtok], inc="dve")
                ngtok = p.op("dve",
                             lambda e: e.tensor_scalar(nmaxb[:, 0:NQT],
                                                       maxf[:, 0:NQT],
                                                       -1.0, None, ALU.mult),
                             waits=[p.now("dve")], inc="dve")
                ttok = p.op("pe",
                            lambda e: e.transpose(pT[0:NQT, :], nmaxb[:, 0:NQT],
                                                  c["ident"][:, :]),
                            waits=[ngtok, p.now("dve")], inc="pe")
                etok = p.op("dve", lambda e: e.tensor_copy(nmrow[0:NQT, :],
                                                           pT[0:NQT, :]),
                            waits=[ttok], inc="dve")
                rwtok = p.dma("sp", trow_d[0:NQT, :], nmrow[0:NQT, :],
                              waits=[etok])
                rwtok = p.dma("sp", negrow[:, 0:S],
                              trow_d[0:NQT, :].rearrange("a b -> (a b)")[None, :],
                              waits=[rwtok])
                for qt in range(NT):
                    cons = [None, None]
                    dvg = p.now("dve")
                    Pdump = [None, None]
                    for kt in range(NKT):
                        pb = kt % 2
                        ps = pA if pb == 0 else pB
                        w = [rwtok] if kt == 0 else []
                        if kt == 0:
                            w.append(dvg)
                        if cons[pb] is not None:
                            w.append(cons[pb])
                        if Pdump[pb] is not None:
                            w.append(Pdump[pb])

                        def mm2(e, h=h, qt=qt, kt=kt, ps=ps):
                            e.matmul(ps[:, :], kh[:, h, kt * 128:(kt + 1) * 128],
                                     qh[:, h, qt * 512:(qt + 1) * 512],
                                     start=True, stop=False)
                            return e.matmul(ps[:, :], c["ones_row"][:, :],
                                            negrow[:, qt * 512:(qt + 1) * 512],
                                            start=False, stop=True)
                        stok = p.op("pe", mm2, waits=w, inc="pe")
                        if (dbg is not None and b == 0 and cn == 0 and h == 0
                                and qt == 0 and kt == 0):
                            dtk = p.op("dve",
                                       lambda e, ps=ps: e.tensor_copy(
                                           x1t[:, 1, :], ps[:, :]),
                                       waits=[stok], inc="dve")
                            p.dma("sp", dbg["sps"], x1t[:, 1, :], waits=[dtk])
                            stok = dtk
                        etok = p.op("act",
                                    lambda e, pb=pb, ps=ps, csap=csap: e.activation(
                                        Pst[pb][:, :], ps[:, :], AF.Exp,
                                        scale=csap),
                                    waits=[stok, sctok], inc="act")
                        if (dbg is not None and b == 0 and cn == 0 and h == 0
                                and qt == 0):
                            ptk = p.dma("pool", dbg["pst"][kt, :, :],
                                        Pst[pb][:, :], waits=[etok])
                            etok = (etok[0], etok[1])
                            cons[pb] = cons[pb]
                            Pdump[pb] = ptk

                        def mm3(e, pb=pb, kt=kt, h=h, first=(kt == 0),
                                last=(kt == NKT - 1)):
                            e.matmul(pD[:, :], c["ones_col"][:, :], Pst[pb][:, :],
                                     start=first, stop=last)
                            return e.matmul(pC[:, :],
                                            vh[:, kt, h * 128:(h + 1) * 128],
                                            Pst[pb][:, :], start=first, stop=last)
                        cons[pb] = p.op("pe", mm3, waits=[etok], inc="pe")
                    def fin(e):
                        e.tensor_copy(srow[:, 0:512], pD[:, :])
                        return e.reciprocal(srow[:, 512:1024], srow[:, 0:512])
                    ftok = p.op("dve", fin, waits=[cons[0], cons[1]], inc="dve")
                    ftok = p.op("act", lambda e: e.copy(nr[:, :], srow[:, 512:1024]),
                                waits=[ftok], inc="act")
                    btok = p.op("pe",
                                lambda e: e.matmul(pb2[:, 0:512], c["ones_row"][:, :],
                                                   nr[:, :], start=True, stop=True),
                                waits=[ftok, p.now("act")], inc="pe")

                    def cevac(e, h=h, qt=qt):
                        dst = bB[:, h, qt * 512:(qt + 1) * 512]
                        e.tensor_copy(dst, pC[:, :])
                        return e.tensor_mul(dst, dst, pb2[:, 0:512])
                    cetok = p.op("dve", cevac, waits=[btok], inc="dve")
                    if dbg is not None and b == 0 and cn == 0 and h == 0 and qt == 0:
                        dtok = p.op("dve",
                                    lambda e: e.tensor_copy(x1t[:, 0, :], pC[:, :]),
                                    waits=[btok], inc="dve")
                        p.dma("sp", dbg["rawctx"], x1t[:, 0, :], waits=[dtok])
                        p.dma("sp", dbg["den"], srow[:, 0:1024], waits=[cetok])
            attdone = [p.now("pe"), p.now("dve"), p.now("act")]
            if dbg is not None and b == 0 and cn == 0:
                p.dma("pool", dbg["ctx"], bB[:, :, :], waits=attdone)
                p.dma("pool", dbg["negrow"], negrow[:, :], waits=attdone)
                p.dma("pool", dbg["scal"], scal[:, :, :], waits=attdone)

            voap = scal[:, cn, 1:2]
            qsap = scal[:, cn, 2:3]
            ndtok = [p.now("d_sp"), p.now("d_sp")]
            for half in range(2):
                wt = load_wslab("f_Wo", cn, half)
                for tt in range(NT):
                  for cl in range(4):
                    co = half * 4 + cl
                    def mmo(e, tt=tt, cl=cl):
                        for cin in range(NCHUNK):
                            i = e.matmul(pX[:, :],
                                         wsb[:, cin, cl * 128:cl * 128 + 128],
                                         bB[:, cin, tt * 512:(tt + 1) * 512],
                                         start=(cin == 0), stop=(cin == NCHUNK - 1))
                        return i
                    mtok = p.op("pe", mmo,
                                waits=attdone + [wt, p.now("dve")], inc="pe")
                    nb = (tt * NCHUNK + co) % 2
                    w = [mtok] + ([ndtok[nb]] if ndtok[nb] else [])

                    def nwrite(e, tt=tt, co=co, nb=nb, voap=voap, qsap=qsap):
                        e.tensor_scalar(ntile[:, nb, :], pX[:, :], voap, None,
                                        ALU.mult)
                        return e.scalar_tensor_tensor(
                            ntile[:, nb, :], qe[:, co, tt * 512:(tt + 1) * 512],
                            qsap, ntile[:, nb, :], ALU.mult, ALU.add)
                    atok = p.op("dve", nwrite, waits=w, inc="dve")
                    ndtok[nb] = p.dma(
                        "sp",
                        node_d[cn][co, :, b * S + tt * 512:b * S + (tt + 1) * 512],
                        ntile[:, nb, :], waits=[atok])
            node_done[cn] = p.now("d_sp")
        nwd = p.now("d_sp")
        for cc in range(NCHUNK):
            p.dma("sp", qh[:, cc, :], node_d[5][cc, :, b * S:(b + 1) * S],
                  waits=[nwd, p.now("pe"), p.now("dve")] if cc == 0 else [])
            p.dma("pool", kh[:, cc, :], xin_d[cc, :, b * S:(b + 1) * S],
                  waits=[p.now("pe"), p.now("dve")] if cc == 0 else [])
        eload = [p.now("d_sp"), p.now("d_pool")]
        for stage in (0, 1):
            prev = [p.now("pe"), p.now("dve"), p.now("act")] + eload
            eps = 1e-6 if stage == 0 else 1e-5
            stok = None
            dmatok = [p.now("d_sp"), p.now("d_sp")]
            for qt in range(NT):
                for co in range(NCHUNK):
                    sqtok = p.op(
                        "act",
                        lambda e, qt=qt, co=co: e.activation(
                            sqt[:, :], qh[:, co, qt * 512:(qt + 1) * 512],
                            AF.Square),
                        waits=prev + ([stok] if stok else []), inc="act")

                    def mmstat(e, qt=qt, co=co):
                        e.matmul(pD[:, :], c["ones_col"][:, :],
                                 qh[:, co, qt * 512:(qt + 1) * 512],
                                 start=(co == 0), stop=(co == NCHUNK - 1))
                        return e.matmul(pb2[0:1, 0:512], c["ones_col"][:, :],
                                        sqt[:, :],
                                        start=(co == 0), stop=(co == NCHUNK - 1))
                    stok = p.op("pe", mmstat, waits=[sqtok], inc="pe")

                def rowmath(e):
                    e.tensor_scalar(srow[:, 0:512], pD[:, :], 1.0 / D, None,
                                    ALU.mult)
                    e.tensor_scalar(srow[:, 512:1024], pb2[0:1, 0:512], 1.0 / D,
                                    None, ALU.mult)
                    e.tensor_mul(srow[:, 1024:1536], srow[:, 0:512],
                                 srow[:, 0:512])
                    return e.tensor_sub(srow[:, 512:1024], srow[:, 512:1024],
                                        srow[:, 1024:1536])
                rtok = p.op("dve", rowmath, waits=[stok], inc="dve")
                bap, btk = c.bias(eps, 1)
                ltok = p.op(
                    "act",
                    lambda e, bap=bap: e.activation(srow[:, 1024:1536],
                                                    srow[:, 512:1024],
                                                    AF.Ln, bias=bap),
                    waits=[rtok, btk], inc="act")

                def rowfin(e):
                    e.mul(mrow[:, 0:512], srow[:, 0:512], -1.0)
                    return e.activation(mrow[:, 512:1024],
                                        srow[:, 1024:1536], AF.Exp, scale=-0.5)
                ftok = p.op("act", rowfin, waits=[ltok], inc="act")
                stok = rtok

                def bcast(e):
                    e.matmul(pA[:, :], c["ones_row"][:, :], mrow[0:1, 0:512],
                             start=True, stop=True)
                    return e.matmul(pB[:, :], c["ones_row"][:, :],
                                    mrow[0:1, 512:1024],
                                    start=True, stop=True)
                btok = p.op("pe", bcast, waits=[ftok, p.now("dve")], inc="pe")
                for co in range(NCHUNK):
                    if stage == 0:
                        def apply0(e, qt=qt, co=co):
                            ysl = qh[:, co, qt * 512:(qt + 1) * 512]
                            e.tensor_add(x1t[:, 0, :], ysl, pA[:, :])
                            e.tensor_mul(x1t[:, 0, :], x1t[:, 0, :], pB[:, :])
                            e.tensor_scalar(x1t[:, 0, :], x1t[:, 0, :],
                                            lncol[:, co, 0:1], lncol[:, co, 1:2],
                                            ALU.mult, ALU.add)
                            return e.tensor_add(ysl, x1t[:, 0, :],
                                                kh[:, co, qt * 512:(qt + 1) * 512])
                        p.op("dve", apply0, waits=[btok], inc="dve")
                    else:
                        xb = co % 2
                        w = [btok] + ([dmatok[xb]] if dmatok[xb] else [])

                        def apply1(e, qt=qt, co=co, xb=xb):
                            ysl = qh[:, co, qt * 512:(qt + 1) * 512]
                            e.tensor_add(x1t[:, xb, :], ysl, pA[:, :])
                            return e.tensor_mul(x1t[:, xb, :], x1t[:, xb, :],
                                                pB[:, :])
                        atok = p.op("dve", apply1, waits=w, inc="dve")
                        dmatok[xb] = p.dma(
                            "sp",
                            xout_d[co, :, b * S + qt * 512:b * S + (qt + 1) * 512],
                            x1t[:, xb, :], waits=[atok])
                stok = (stok[0], stok[1])
                prev = prev + [p.now("dve")]
    return


def prep_l0_ffn_weights(inp):
    We = np.asarray(inp["L0_ffn_We"], np.float32)
    out = {
        "f_We": np.ascontiguousarray(We.reshape(20, 8, 128, D)),
        "f_Wq": np.ascontiguousarray(
            np.asarray(inp["L0_ffn_Wq"], np.float32).reshape(6, 8, 128, D)),
        "f_Wk": np.ascontiguousarray(
            np.asarray(inp["L0_ffn_Wk"], np.float32).reshape(6, 8, 128, D)),
        "f_Wv": np.ascontiguousarray(
            np.asarray(inp["L0_ffn_Wv"], np.float32).reshape(6, 8, 128, D)),
        "f_Wo": np.ascontiguousarray(
            np.asarray(inp["L0_ffn_Wo"], np.float32).reshape(6, 8, 128, D)),
        "f_becol": np.ascontiguousarray(
            np.asarray(inp["L0_ffn_be"], np.float32).reshape(20, 8, 128)
            .transpose(2, 0, 1)),
        "f_lncol": np.ascontiguousarray(np.stack(
            [np.asarray(inp["L0_ffn_ln_g"], np.float32).reshape(8, 128),
             np.asarray(inp["L0_ffn_ln_b"], np.float32).reshape(8, 128)],
            axis=-1).transpose(1, 0, 2).reshape(128, 8, 2)),
        "f_ewact_bc": np.ascontiguousarray(np.tile(np.concatenate(
            [np.asarray(inp["L0_ffn_ew"], np.float32).reshape(-1),
             np.asarray(inp["L0_ffn_act_w"], np.float32).reshape(-1)]
        ).reshape(1, 24), (128, 1))),
    }
    return out


def run_l0_ffn(x1_full, inp, S=2048, debug=False):
    ncores = 8
    host_w = prep_l0_ffn_weights(inp)
    nc = bass.Bass("TRN2", target_bir_lowering=False)
    xin_d = nc.dram_tensor("x_in", [NCHUNK, 128, 2 * S], F32, kind="ExternalInput").ap()
    xout_d = nc.dram_tensor("x_out", [NCHUNK, 128, 2 * S], F32,
                            kind="ExternalOutput").ap()
    kind = "ExternalOutput" if debug else "Internal"
    node_d = [nc.dram_tensor("node%d" % i, [NCHUNK, 128, 2 * S], BF16,
                             kind=kind).ap() for i in range(6)]
    dbg = None
    if debug:
        dbg = {
            "qe": nc.dram_tensor("dbg_qe", [128, NCHUNK, S], F32, kind="ExternalOutput").ap(),
            "qh": nc.dram_tensor("dbg_qh", [128, NCHUNK, S], F32, kind="ExternalOutput").ap(),
            "kh": nc.dram_tensor("dbg_kh", [128, NCHUNK, S], F32, kind="ExternalOutput").ap(),
            "vh": nc.dram_tensor("dbg_vh", [128, S // 128, D], F32, kind="ExternalOutput").ap(),
            "ctx": nc.dram_tensor("dbg_ctx", [128, NCHUNK, S], F32, kind="ExternalOutput").ap(),
            "negrow": nc.dram_tensor("dbg_negrow", [1, S], F32, kind="ExternalOutput").ap(),
            "scal": nc.dram_tensor("dbg_scal", [128, 6, 3], F32, kind="ExternalOutput").ap(),
            "rawctx": nc.dram_tensor("dbg_rawctx", [128, 512], F32, kind="ExternalOutput").ap(),
            "den": nc.dram_tensor("dbg_den", [1, 1024], F32, kind="ExternalOutput").ap(),
            "sps": nc.dram_tensor("dbg_sps", [128, 512], F32, kind="ExternalOutput").ap(),
            "pst": nc.dram_tensor("dbg_pst", [16, 128, 512], F32, kind="ExternalOutput").ap(),
        }
    p = Prog(nc)
    host_c = {}
    with ExitStack() as es:
        wd = declare_weights(nc, host_w)
        cst = alloc_consts(nc, es, p, host_c)
        build_l0_ffn(nc, es, p, cst, xin_d, xout_d, node_d, wd, S, dbg=dbg)
        p.op("sp", lambda e: e.nop(), waits=p.all_now())
        p.build(es)

    in_maps = []
    for i in range(ncores):
        xi = np.concatenate([x1_full[2 * i], x1_full[2 * i + 1]], 0)
        m = {"x_in": fm_slab(xi)}
        m.update(host_w)
        m.update(host_c)
        in_maps.append(m)
    res = run_bass_kernel_spmd(nc, in_maps, core_ids=list(range(ncores)))
    outs = []
    for i in range(ncores):
        o = from_fm_slab(res.results[i]["x_out"])
        outs.append(o[:S])
        outs.append(o[S:])
    if debug:
        return np.stack(outs), res.results
    return np.stack(outs)



def prep_l0_attn_weights(inp):
    wqkv = np.asarray(inp["L0_attn_w_qkv"], np.float32)
    qc = np.concatenate([wqkv[:, h * 192:h * 192 + 64] for h in range(16)], 1)
    kc = np.concatenate([wqkv[:, h * 192 + 64:h * 192 + 128] for h in range(16)], 1)
    vc = np.concatenate([wqkv[:, h * 192 + 128:h * 192 + 192] for h in range(16)], 1)
    out = {
        "a_wq": w_slab(qc),
        "a_wk": w_slab(kc),
        "a_wv": w_slab(vc),
        "a_wo": w_slab(np.asarray(inp["L0_attn_w_out"], np.float32)),
        "a_bcol": np.ascontiguousarray(np.concatenate(
            [col(inp["L0_attn_b_out"]), col(inp["L0_attn_gamma"]),
             col(inp["L0_attn_beta"])], 1).reshape(NCHUNK, 128, 3)),
    }
    return out


def declare_weights(nc, host_w):
    d = {}
    for name, arr in host_w.items():
        d[name] = nc.dram_tensor(name, list(arr.shape), F32, kind="ExternalInput").ap()
    return d



def run_l0_attn(x_full, inp, S=2048, debug=False):
    ncores = 8
    host_w = prep_l0_attn_weights(inp)
    nc = bass.Bass("TRN2", target_bir_lowering=False)
    xin_d = nc.dram_tensor("x_in", [NCHUNK, 128, 2 * S], F32, kind="ExternalInput").ap()
    xout_d = nc.dram_tensor("x_out", [NCHUNK, 128, 2 * S], F32, kind="ExternalOutput").ap()
    dbg = None
    if debug:
        dbg = {
            "qg": nc.dram_tensor("dbg_qg", [128, 2, S], F32, kind="ExternalOutput").ap(),
            "kg": nc.dram_tensor("dbg_kg", [128, 2, S], F32, kind="ExternalOutput").ap(),
            "vgt": nc.dram_tensor("dbg_vgt", [128, S // 128, 256], F32,
                                  kind="ExternalOutput").ap(),
            "ctx": nc.dram_tensor("dbg_ctx", [128, NCHUNK, S], F32,
                                  kind="ExternalOutput").ap(),
            "ybuf": nc.dram_tensor("dbg_ybuf", [128, NCHUNK, S], F32,
                                   kind="ExternalOutput").ap(),
            "mrow0": nc.dram_tensor("dbg_mrow0", [1, 2 * (S // 512) * 512], F32,
                                    kind="ExternalOutput").ap(),
            "mrow1": nc.dram_tensor("dbg_mrow1", [1, 2 * (S // 512) * 512], F32,
                                    kind="ExternalOutput").ap(),
            "z": nc.dram_tensor("dbg_z", [128, NCHUNK, S], F32,
                                kind="ExternalOutput").ap(),
        }
    p = Prog(nc)
    host_c = {}
    with ExitStack() as es:
        wd = declare_weights(nc, host_w)
        c = alloc_consts(nc, es, p, host_c)
        build_l0_attn(nc, es, p, c, xin_d, xout_d, wd, S, dbg=dbg)
        p.op("sp", lambda e: e.nop(), waits=p.all_now())
        p.build(es)

    in_maps = []
    for i in range(ncores):
        xi = np.concatenate([x_full[2 * i], x_full[2 * i + 1]], 0)
        m = {"x_in": fm_slab(xi)}
        m.update(host_w)
        m.update(host_c)
        in_maps.append(m)
    res = run_bass_kernel_spmd(nc, in_maps, core_ids=list(range(ncores)))
    outs = []
    for i in range(ncores):
        o = from_fm_slab(res.results[i]["x_out"])
        outs.append(o[:S])
        outs.append(o[S:])
    if debug:
        return np.stack(outs), res.results
    return np.stack(outs)



def build_l1_attn(nc, es, p, c, xin_d, xout_d, sl, wd, S, dbg=None):
    NB = 2
    H = 16
    L = 64
    NCH = S // L
    NT = S // 512
    NTI = S // 128

    def sbuf(name, shape, dtype):
        return es.enter_context(nc.sbuf_tensor("lb_" + name, shape, dtype))

    def psum(name, shape, dtype=F32):
        return es.enter_context(nc.psum_tensor("lp_" + name, shape, dtype))

    xin = sbuf("xin", [128, NCHUNK, S], BF16)
    dxb = sbuf("dxb", [128, NCHUNK, S], BF16)
    xdb = sbuf("xdb", [128, NCHUNK, S], BF16)
    knb = sbuf("knb", [128, NCHUNK, S], BF16)

    uLR = sbuf("uLR", [128, S], BF16)
    wsb = sbuf("wsb", [128, NCHUNK, 512], BF16)
    cols = sbuf("cols", [128, NCHUNK, 12], F32)
    stg = sbuf("stg", [128, 4, 512], BF16)
    stg32 = sbuf("stg32", [128, 2, 512], F32)
    nr = sbuf("nr", [1, 512], BF16)
    masks = sbuf("masks", [64, 192], BF16)
    ones64 = sbuf("ones64", [64, 64], BF16)
    hR = sbuf("hR", [64, S], BF16)
    hW = sbuf("hW", [64, S], F32)
    hKK = sbuf("hKK", [64, S], BF16)
    hBB = sbuf("hBB", [64, S], BF16)
    hKN = sbuf("hKN", [64, S], BF16)
    hVT = sbuf("hVT", [64, NCH, 64], BF16)
    hY = sbuf("hY", [64, S], BF16)
    cwt = sbuf("cwt", [64, L], F32)
    F1 = sbuf("F1", [64, L], BF16)
    F2 = sbuf("F2", [64, L], BF16)
    F3 = sbuf("F3", [64, L], BF16)
    ccol = sbuf("ccol", [64, 8], F32)
    Aq = sbuf("Aq", [64, L], BF16)
    Bq = sbuf("Bq", [64, L], BF16)
    Kq = sbuf("Kq", [64, L], BF16)
    Rq = sbuf("Rq", [64, L], BF16)
    NTs = sbuf("NTs", [64, L], BF16)
    NTTs = sbuf("NTTs", [64, L], BF16)
    Mks = sbuf("Mks", [64, L], BF16)
    Sbs = sbuf("Sbs", [64, L], BF16)
    Sks = sbuf("Sks", [64, L], BF16)
    Zb = [sbuf("Zb%d" % i, [64, L], BF16) for i in range(2)]
    Pb = [sbuf("Pb%d" % i, [64, L], BF16) for i in range(2)]
    PTb = [sbuf("PTb%d" % i, [64, L], BF16) for i in range(2)]
    S0M = sbuf("S0M", [64, L], BF16)
    Stt = sbuf("Stt", [64, L], BF16)
    BfT = sbuf("BfT", [64, L], BF16)
    KfT = sbuf("KfT", [64, L], BF16)
    Bf = sbuf("Bf", [64, L], BF16)
    Kf = sbuf("Kf", [64, L], BF16)
    diagm = sbuf("diagm", [64, L], BF16)
    hind = sbuf("hind", [128, 2], BF16)
    hindT = sbuf("hindT", [2, 128], BF16)
    g2row = sbuf("g2row", [2, 1024], BF16)
    g32 = sbuf("g32", [2, 1536], F32)
    srow = g32
    mrow = g2row
    sqt = sbuf("sqt", [128, 512], BF16)
    x1t = sbuf("x1t", [128, 2, 512], F32)
    xB1 = sbuf("xB1", [32, 1024], BF16)
    mxt = sbuf("mxt", [32, 2, 512], BF16)

    pX = psum("pX", [128, 512])
    pY = psum("pY", [128, 512])
    pm = [psum("pm%d" % i, [64, 512]) for i in range(5)]
    pTr = psum("pTr", [64, 64], BF16)

    p.dma("sp", cols[:, :, :], wd["l_cols"])
    p.dma("pool", masks[:, :], wd["l_masks"])
    ctok = p.op("pool", lambda e: e.memset(ones64[:, :], 1.0), inc="c0")
    p.dma("pool", hind[:, :], wd["l_hind"])
    p.dma("pool", hindT[:, :], wd["l_hindT"])
    cload = [p.now("d_sp"), p.now("d_pool"), ctok]

    LAM, XMIU, WMIU, ATIME, GNG, GNB, XTIME = 0, 6, 7, 8, 9, 10, 11
    mask_lowS = masks[:, 0:64]
    mask_upS = masks[:, 64:128]
    mask_upI = masks[:, 128:192]

    def load_wslab(name, half, guard=()):
        g = [p.now("pe")] + list(guard)
        for cc in range(NCHUNK):
            p.dma("pool", wsb[:, cc, :],
                  wd[name][cc, :, half * 512:(half + 1) * 512], waits=g)
        return p.now("d_pool")

    def load_lr(nameA, nameB, ka):
        g = [p.now("pe")]
        for cc in range(NCHUNK):
            p.dma("pool", wsb[:, cc, 0:ka], wd[nameA][cc, :, :], waits=g)
        p.dma("pool", lrB[0:ka, :], wd[nameB], waits=g)
        return p.now("d_pool")

    lrB = sbuf("lrB", [128, 1024], BF16)

    for b in range(NB):
        t0 = b * S
        xw = [p.now("pe"), p.now("dve"), p.now("act"),
              p.now("d_sp")] if b > 0 else []
        for cc in range(NCHUNK):
            p.dma("pool", xin[:, cc, :], xin_d[cc, :, t0:t0 + S], waits=xw)
        xload = [p.now("d_pool")] + cload

        for cc in range(NCHUNK):
            def mkdx(e, cc=cc):
                e.tensor_sub(dxb[:, cc, 1:S], xin[:, cc, 0:S - 1],
                             xin[:, cc, 1:S])
                e.tensor_scalar(dxb[:, cc, 0:1], xin[:, cc, 0:1], -1.0, None,
                                ALU.mult)
                return e.scalar_tensor_tensor(
                    xdb[:, cc, :], dxb[:, cc, :], cols[:, cc, XMIU:XMIU + 1],
                    xin[:, cc, :], ALU.mult, ALU.add)
            p.op("dve", mkdx, waits=xload, inc="dve")

        for half in range(2):
            g_ = [p.now("pe")] + xload
            for cc in range(NCHUNK):
                p.dma("pool", wsb[:, cc, 0:96],
                      wd["l_xA%d" % half][cc, :, :], waits=g_)
            wt = p.now("d_pool")
            for tt in range(NT):
                def mmx(e, tt=tt):
                    for cin in range(NCHUNK):
                        i = e.matmul(pX[0:96, :], wsb[:, cin, 0:96],
                                     xdb[:, cin, tt * 512:(tt + 1) * 512],
                                     start=(cin == 0), stop=(cin == NCHUNK - 1))
                    return i
                mtok = p.op("pe", mmx, waits=[wt, p.now("act"), p.now("dve")],
                            inc="pe")
                etok = p.op("act",
                            lambda e: e.activation(stg[0:96, 0, :],
                                                   pX[0:96, :], AF.Tanh),
                            waits=[mtok, p.now("d_sp")], inc="act")
                p.dma("sp", sl["mix"][half * 96:(half + 1) * 96,
                                      t0 + tt * 512:t0 + (tt + 1) * 512],
                      stg[0:96, 0, :], waits=[etok])
        mixdone = [p.now("pe"), p.now("act"), p.now("dve"), p.now("d_sp")]

        def make_xd(g):
            p.dma("pool", xB1[:, :], wd["l_xB"][g], waits=[p.now("pe")])
            xbl = p.now("d_pool")
            mtk = {}
            for tt in range(NT):
                mw = [mtk[tt % 2]] if tt % 2 in mtk else []
                mld = p.dma("sp", mxt[:, tt % 2, :],
                            sl["mix"][g * 32:(g + 1) * 32,
                                      t0 + tt * 512:t0 + (tt + 1) * 512],
                            waits=mixdone + mw)
                for cc in range(NCHUNK):
                    def mmg(e, cc=cc, tt=tt):
                        return e.matmul(
                            pY[:, :], xB1[:, cc * 128:(cc + 1) * 128],
                            mxt[:, tt % 2, tt * 512 - tt * 512:512],
                            start=True, stop=True)
                    mtok = p.op("pe", mmg,
                                waits=mixdone + [p.now("dve"), p.now("act"),
                                                 xbl, mld], inc="pe")
                    mtk[tt % 2] = mtok
                    etok = p.op("act",
                                lambda e, cc=cc, g=g: e.activation(
                                    stg[:, 2, :], pY[:, :], AF.Identity,
                                    bias=cols[:, cc, LAM + g:LAM + g + 1]),
                                waits=[mtok], inc="act")

                    def fxd(e, cc=cc, tt=tt):
                        sl_ = slice(tt * 512, (tt + 1) * 512)
                        e.tensor_mul(stg[:, 3, :], stg[:, 2, :], dxb[:, cc, sl_])
                        return e.tensor_add(xdb[:, cc, sl_], stg[:, 3, :],
                                            xin[:, cc, sl_])
                    p.op("dve", fxd, waits=[etok, p.now("pe")], inc="dve")
            return [p.now("pe"), p.now("dve"), p.now("act")]

        def proj_full(wname, dst_slab, src=None, extra_lr=None):
            src = xdb if src is None else src
            dtoks = {0: p.now("d_sp"), 1: p.now("d_sp"),
                     2: p.now("d_sp"), 3: p.now("d_sp")}
            for half in range(2):
                wt = load_wslab(wname, half)
                for tt in range(NT):
                    for cl in range(4):
                        co = half * 4 + cl

                        def mmg2(e, tt=tt, cl=cl):
                            for cin in range(NCHUNK):
                                i = e.matmul(pX[:, :],
                                             wsb[:, cin, cl * 128:cl * 128 + 128],
                                             src[:, cin, tt * 512:(tt + 1) * 512],
                                             start=(cin == 0),
                                             stop=(cin == NCHUNK - 1))
                            return i

                        def mmg3(e, tt=tt, cl=cl, co=co, ub=None, ka=0):
                            for cin in range(NCHUNK):
                                e.matmul(pX[:, :],
                                         wsb[:, cin, cl * 128:cl * 128 + 128],
                                         src[:, cin, tt * 512:(tt + 1) * 512],
                                         start=(cin == 0), stop=False)
                            return e.matmul(
                                pX[:, :], lrB[0:ka, co * 128:co * 128 + 128],
                                ub[0:ka, tt * 512:(tt + 1) * 512],
                                start=False, stop=True)
                        if extra_lr is None:
                            fn = mmg2
                        else:
                            ubx, kax = extra_lr
                            def fn(e, f=mmg3, tt=tt, cl=cl, co=co, ub=ubx,
                                   ka=kax):
                                return f(e, tt=tt, cl=cl, co=co, ub=ub, ka=ka)
                        mtok = p.op("pe", fn,
                                    waits=[wt, p.now("act"), p.now("dve")],
                                    inc="pe")
                        sb = (tt * 8 + co) % 4
                        wdma = [dtoks[sb]] if sb in dtoks else []
                        etok = p.op("dve",
                                    lambda e, sb=sb: e.tensor_copy(
                                        stg[:, sb, :], pX[:, :]),
                                    waits=[mtok] + wdma, inc="dve")
                        dtoks[sb] = p.dma(
                            "sp", dst_slab[co, :, t0 + tt * 512:t0 + (tt + 1) * 512],
                            stg[:, sb, :], waits=[etok])
            return p.now("d_sp")

        def lowrank_u(nameA, ka, ub, act=AF.Tanh):
            g = [p.now("pe")]
            for cc in range(NCHUNK):
                p.dma("pool", wsb[:, cc, 0:ka], wd[nameA][cc], waits=g)
            wt = p.now("d_pool")
            for tt in range(NT):
                def mmu(e, tt=tt, ka=ka):
                    for cin in range(NCHUNK):
                        i = e.matmul(pY[0:ka, :], wsb[:, cin, 0:ka],
                                     xdb[:, cin, tt * 512:(tt + 1) * 512],
                                     start=(cin == 0), stop=(cin == NCHUNK - 1))
                    return i
                mtok = p.op("pe", mmu, waits=[wt, p.now("act")], inc="pe")
                p.op("act",
                     lambda e, tt=tt, ka=ka, ub=ub, act=act: e.activation(
                         ub[0:ka, tt * 512:(tt + 1) * 512], pY[0:ka, :], act),
                     waits=[mtok], inc="act")
            return [p.now("pe"), p.now("act")]

        make_xd(0)
        proj_full("l_rW", sl["r"])
        make_xd(1)
        lowrank_u("l_wA", 64, uLR)
        gb = [p.now("pe")]
        p.dma("pool", lrB[0:64, :], wd["l_wB"], waits=gb)
        wt = p.now("d_pool")
        wdt = {0: p.now("d_sp")}
        for cc in range(NCHUNK):
            for tt in range(NT):
                def mmw(e, cc=cc, tt=tt):
                    return e.matmul(pX[:, :], lrB[0:64, cc * 128:cc * 128 + 128],
                                    uLR[0:64, tt * 512:(tt + 1) * 512],
                                    start=True, stop=True)
                mtok = p.op("pe", mmw,
                            waits=[wt, p.now("act"), p.now("dve")], inc="pe")
                etok = p.op("act",
                            lambda e, cc=cc: e.activation(
                                stg[:, 0, :], pX[:, :], AF.Identity,
                                bias=cols[:, cc, WMIU:WMIU + 1]),
                            waits=[mtok], inc="act")
                bap1, btk1 = c.bias(1.0, 128)
                def wchain2(e, bap1=bap1):
                    e.activation(stg[:, 1, :], stg[:, 0, :], AF.Abs)
                    e.activation(stg[:, 1, :], stg[:, 1, :], AF.Exp, scale=-1.0)
                    return e.activation(stg[:, 1, :], stg[:, 1, :], AF.Ln,
                                        bias=bap1)
                atok = p.op("act", wchain2, waits=[etok, btk1], inc="act")

                def wfin(e, cc=cc, tt=tt):
                    e.tensor_scalar(stg32[:, 0, :], stg[:, 0, :], -1.0, 0.0,
                                    ALU.mult, ALU.max)
                    e.tensor_add(stg32[:, 0, :], stg32[:, 0, :], stg[:, 1, :])
                    return e.tensor_scalar(stg32[:, 1, :], stg32[:, 0, :],
                                           -1.0, -0.5, ALU.mult, ALU.add)
                ftok = p.op("dve", wfin, waits=[atok] + (
                    [wdt[0]] if 0 in wdt else []), inc="dve")
                wdt[0] = p.dma(
                    "sp", sl["w"][cc, :, t0 + tt * 512:t0 + (tt + 1) * 512],
                    stg32[:, 1, :], waits=[ftok])
        wslabdone = p.now("d_sp")
        make_xd(2)
        lowrank_u("l_kA", 64, uLR)
        p.dma("pool", lrB[0:64, :], wd["l_kB"], waits=[p.now("pe")])
        for half in range(2):
            wt = load_wslab("l_kW", half)
            for tt in range(NT):
                for cl in range(4):
                    co = half * 4 + cl

                    def mmk(e, tt=tt, cl=cl, co=co):
                        for cin in range(NCHUNK):
                            e.matmul(pX[:, :],
                                     wsb[:, cin, cl * 128:cl * 128 + 128],
                                     xdb[:, cin, tt * 512:(tt + 1) * 512],
                                     start=(cin == 0), stop=False)
                        return e.matmul(pX[:, :],
                                        lrB[0:64, co * 128:co * 128 + 128],
                                        uLR[0:64, tt * 512:(tt + 1) * 512],
                                        start=False, stop=True)
                    mtok = p.op("pe", mmk,
                                waits=[wt, p.now("act"), p.now("dve")], inc="pe")
                    p.op("dve",
                         lambda e, tt=tt, co=co: e.tensor_copy(
                             knb[:, co, tt * 512:(tt + 1) * 512], pX[:, :]),
                         waits=[mtok], inc="dve")
        krdone = [p.now("pe"), p.now("dve")]
        bap12, btk12 = c.bias(1e-12, 1)
        k2tok = {0: p.now("d_sp"), 1: p.now("d_sp")}
        k4tok = ("dve", 0)
        for tt in range(NT):
            stok = None
            for co in range(NCHUNK):
                sqtok = p.op("act",
                             lambda e, tt=tt, co=co: e.activation(
                                 sqt[:, :], knb[:, co, tt * 512:(tt + 1) * 512],
                                 AF.Square),
                             waits=krdone + ([stok] if stok else []), inc="act")

                def mmss(e, co=co):
                    return e.matmul(pm[0][0:1, :], c["ones_col"][:, :],
                                    sqt[:, :], start=(co == 0),
                                    stop=(co == NCHUNK - 1))
                stok = p.op("pe", mmss, waits=[sqtok], inc="pe")

            rt1 = p.op("act", lambda e: e.copy(srow[0:1, 0:512],
                                               pm[0][0:1, :]),
                       waits=[stok], inc="act")
            rt1 = p.op("act", lambda e: e.activation(srow[0:1, 512:1024],
                                                     srow[0:1, 0:512], AF.Ln),
                       waits=[rt1], inc="act")

            def knorm_row2(e):
                return e.activation(srow[0:1, 512:1024], srow[0:1, 512:1024],
                                    AF.Exp, scale=0.5)
            rt2 = p.op("act", knorm_row2, waits=[rt1], inc="act")

            def knorm_row3(e, bap12=bap12):
                e.tensor_scalar(srow[0:1, 512:1024], srow[0:1, 512:1024], 1e-12,
                                None, ALU.max)
                return e.reciprocal(srow[0:1, 1024:1536], srow[0:1, 512:1024])
            rt3 = p.op("dve", knorm_row3, waits=[rt2, btk12], inc="dve")
            rt4 = p.op("act", lambda e: e.copy(nr[:, :], srow[0:1, 1024:1536]),
                       waits=[rt3], inc="act")
            btok = p.op("pe",
                        lambda e: e.matmul(pY[:, :], c["ones_row"][:, :],
                                           nr[:, :], start=True, stop=True),
                        waits=[rt4], inc="pe")
            for co in range(NCHUNK):
                wl = p.dma("pool", stg32[:, co % 2, :],
                           sl["w"][co, :, t0 + tt * 512:t0 + (tt + 1) * 512],
                           waits=[wslabdone] + (
                               [k2tok[co % 2]] if co % 2 in k2tok else []))

                def knf(e, tt=tt, co=co):
                    ksl = knb[:, co, tt * 512:(tt + 1) * 512]
                    return e.tensor_mul(ksl, ksl, pY[:, :])
                ktok = p.op("dve", knf, waits=[btok], inc="dve")
                k2 = p.op("dve", lambda e, co=co: e.tensor_scalar(
                    stg[:, 2, :], stg32[:, co % 2, :], 0.5, 0.0,
                    ALU.mult, ALU.min), waits=[wl, ktok, k4tok], inc="dve")
                k2tok[co % 2] = k2
                k3 = p.op("act", lambda e: e.activation(stg[:, 2, :],
                                                        stg[:, 2, :], AF.Exp),
                          waits=[k2], inc="act")
                k4 = p.op("dve",
                          lambda e, tt=tt, co=co: e.tensor_mul(
                              stg[:, 3, :], knb[:, co, tt * 512:(tt + 1) * 512],
                              stg[:, 2, :]),
                          waits=[k3], inc="dve")
                k4tok = k4
                p.dma("sp", sl["kk"][co, :, t0 + tt * 512:t0 + (tt + 1) * 512],
                      stg[:, 3, :], waits=[k4])
                p.dma("sp", sl["kn"][co, :, t0 + tt * 512:t0 + (tt + 1) * 512],
                      knb[:, co, tt * 512:(tt + 1) * 512], waits=[k4])
        kndone = [p.now("pe"), p.now("dve"), p.now("act"), p.now("d_sp")]
        make_xd(3)
        proj_full("l_vW", sl["vfm"])
        for half in range(2):
            wt = load_wslab("l_vW", half)
            vdt = {0: p.now("d_sp"), 1: p.now("d_sp"),
                   2: p.now("d_sp"), 3: p.now("d_sp")}
            for ti in range(NTI):
                def mmv(e, ti=ti):
                    for cin in range(NCHUNK):
                        i = e.matmul(pX[:, :],
                                     xdb[:, cin, ti * 128:(ti + 1) * 128],
                                     wsb[:, cin, :],
                                     start=(cin == 0), stop=(cin == NCHUNK - 1))
                    return i
                mtok = p.op("pe", mmv, waits=[wt, p.now("dve")], inc="pe")
                sb = ti % 4
                wdma = [vdt[sb]] if sb in vdt else []
                etok = p.op("dve",
                            lambda e, sb=sb: e.tensor_copy(stg[:, sb, :],
                                                           pX[:, :]),
                            waits=[mtok] + wdma, inc="dve")
                vdt[sb] = p.dma(
                    "sp",
                    sl["vtm"][b * NTI + ti, :, half * 512:(half + 1) * 512],
                    stg[:, sb, :], waits=[etok])
        make_xd(4)
        lowrank_u("l_aA", 64, uLR, act=AF.Identity)
        p.dma("pool", lrB[0:64, :], wd["l_aB"], waits=[p.now("pe")])
        wt = p.now("d_pool")
        adt = {0: p.now("d_sp"), 1: p.now("d_sp")}
        for cc in range(NCHUNK):
            for tt in range(NT):
                def mma(e, cc=cc, tt=tt):
                    return e.matmul(pX[:, :], lrB[0:64, cc * 128:cc * 128 + 128],
                                    uLR[0:64, tt * 512:(tt + 1) * 512],
                                    start=True, stop=True)
                mtok = p.op("pe", mma,
                            waits=[wt, p.now("act"), p.now("dve")], inc="pe")
                etok = p.op("act",
                            lambda e, cc=cc: e.activation(
                                stg[:, 0, :], pX[:, :], AF.Sigmoid,
                                bias=cols[:, cc, ATIME:ATIME + 1]),
                            waits=[mtok], inc="act")
                sb = (cc * NT + tt) % 2
                wdma = [adt[sb]] if sb in adt else []

                def bbf(e, cc=cc, tt=tt, sb=sb):
                    return e.scalar_tensor_tensor(
                        stg[:, 2 + sb, :], knb[:, cc, tt * 512:(tt + 1) * 512],
                        2.0, stg[:, 0, :], ALU.mult, ALU.mult)
                atok = p.op("dve", bbf, waits=[etok] + wdma, inc="dve")
                adt[sb] = p.dma(
                    "sp", sl["bb"][cc, :, t0 + tt * 512:t0 + (tt + 1) * 512],
                    stg[:, 2 + sb, :], waits=[atok])
        make_xd(5)
        lowrank_u("l_gA", 128, uLR)
        p.dma("pool", lrB[:, :], wd["l_gB"], waits=[p.now("pe")])
        wt = p.now("d_pool")
        gdt = {0: p.now("d_sp"), 1: p.now("d_sp")}
        for cc in range(NCHUNK):
            for tt in range(NT):
                def mmg(e, cc=cc, tt=tt):
                    return e.matmul(pX[:, :], lrB[:, cc * 128:cc * 128 + 128],
                                    uLR[:, tt * 512:(tt + 1) * 512],
                                    start=True, stop=True)
                mtok = p.op("pe", mmg,
                            waits=[wt, p.now("act"), p.now("dve")], inc="pe")
                sb = (cc * NT + tt) % 2
                wdma = [gdt[sb]] if sb in gdt else []
                etok = p.op("dve",
                            lambda e, sb=sb: e.tensor_copy(stg[:, sb, :],
                                                           pX[:, :]),
                            waits=[mtok] + wdma, inc="dve")
                gdt[sb] = p.dma(
                    "sp", sl["gg"][cc, :, t0 + tt * 512:t0 + (tt + 1) * 512],
                    stg[:, sb, :], waits=[etok])
        prepdone = [p.now("pe"), p.now("dve"), p.now("act"), p.now("d_sp"),
                    p.now("d_pool")]

        for h in range(H):
            cc, rb = h // 2, 64 * (h % 2)
            hw = prepdone if h == 0 else [p.now("pe"), p.now("dve"),
                                          p.now("d_sp")]
            p.dma("sp", hR[:, :], sl["r"][cc, rb:rb + 64, t0:t0 + S], waits=hw)
            p.dma("sp", hW[:, :], sl["w"][cc, rb:rb + 64, t0:t0 + S])
            p.dma("sp", hKK[:, :], sl["kk"][cc, rb:rb + 64, t0:t0 + S])
            p.dma("sp", hBB[:, :], sl["bb"][cc, rb:rb + 64, t0:t0 + S])
            p.dma("sp", hKN[:, :], sl["kn"][cc, rb:rb + 64, t0:t0 + S])
            p.dma("sp", hVT[:, :, :],
                  sl["vtm"][b * NTI:(b + 1) * NTI, :, h * 64:h * 64 + 64]
                  .rearrange("a (c p) d -> p (a c) d", c=2))
            hload = p.now("d_sp")
            sttok = p.op("dve", lambda e: e.memset(Stt[:, :], 0.0), inc="dve")
            ydone = None
            for ci in range(NCH):
                csl = slice(ci * L, (ci + 1) * L)
                vti = hVT[:, ci, :]
                cwtok = p.op("dve",
                             lambda e, csl=csl: e.tensor_tensor_scan(
                                 cwt[:, :], ones64[:, :], hW[:, csl], 0.0,
                                 ALU.mult, ALU.add),
                             waits=[hload], inc="dve")

                def mkcol(e):
                    e.tensor_scalar(ccol[:, 0:1], cwt[:, 31:32], -1.0, None,
                                    ALU.mult)
                    return e.tensor_copy(ccol[:, 4:5], cwt[:, 63:64])
                cctok = p.op("dve", mkcol, waits=[cwtok], inc="dve")

                def mkf(e):
                    e.activation(F1[:, 1:L], cwt[:, 0:L - 1], AF.Exp,
                                 bias=ccol[:, 0:1])
                    e.activation(F1[:, 0:1], ccol[:, 0:1], AF.Exp)
                    e.activation(F2[:, :], cwt[:, :], AF.Exp, scale=-1.0,
                                 bias=cwt[:, 31:32])
                    e.activation(F3[:, :], cwt[:, :], AF.Exp, bias=ccol[:, 0:1])
                    e.activation(ccol[:, 1:2], cwt[:, 31:32], AF.Exp)
                    e.activation(ccol[:, 2:3], ccol[:, 4:5], AF.Exp,
                                 bias=ccol[:, 0:1])
                    return e.activation(ccol[:, 3:4], ccol[:, 4:5], AF.Exp)
                ftok = p.op("act", mkf, waits=[cctok], inc="act")

                def mkq(e, csl=csl):
                    e.scalar_tensor_tensor(Aq[:, :], hKN[:, csl], -1.0,
                                           F1[:, :], ALU.mult, ALU.mult)
                    e.tensor_mul(Bq[:, :], hBB[:, csl], F2[:, :])
                    e.tensor_mul(Kq[:, :], hKK[:, csl], F2[:, :])
                    e.tensor_mul(Rq[:, :], hR[:, csl], F3[:, :])
                    e.tensor_scalar(S0M[:, :], Stt[:, :], ccol[:, 1:2], None,
                                    ALU.mult)
                    e.scalar_tensor_tensor(Bf[:, :], hBB[:, csl],
                                           ccol[:, 2:3], F2[:, :],
                                           ALU.mult, ALU.mult)
                    e.scalar_tensor_tensor(Kf[:, :], hKK[:, csl],
                                           ccol[:, 2:3], F2[:, :],
                                           ALU.mult, ALU.mult)
                    return e.tensor_scalar(diagm[:, :], c["ident"][0:64, 0:64],
                                           ccol[:, 3:4], None, ALU.mult)
                qtok = p.op("dve", mkq, waits=[ftok] + (
                    [ydone] if ydone else []), inc="dve")
                m1 = p.op("pe", lambda e: e.matmul(pm[0][:, 0:64], Aq[:, :],
                                                   Bq[:, :], start=True,
                                                   stop=True),
                          waits=[qtok], inc="pe")
                e1 = p.op("dve", lambda e: e.tensor_mul(NTs[:, :],
                                                        pm[0][:, 0:64],
                                                        mask_lowS),
                          waits=[m1], inc="dve")
                m2 = p.op("pe", lambda e: e.matmul(pm[1][:, 0:64], Bq[:, :],
                                                   Aq[:, :], start=True,
                                                   stop=True),
                          waits=[qtok], inc="pe")
                e2 = p.op("dve", lambda e: e.tensor_mul(NTTs[:, :],
                                                        pm[1][:, 0:64],
                                                        mask_upS),
                          waits=[m2], inc="dve")
                m3 = p.op("pe", lambda e: e.matmul(pm[2][:, 0:64], Kq[:, :],
                                                   Aq[:, :], start=True,
                                                   stop=True),
                          waits=[qtok, e1], inc="pe")
                e3 = p.op("dve", lambda e: e.tensor_mul(Mks[:, :],
                                                        pm[2][:, 0:64],
                                                        mask_upS),
                          waits=[m3], inc="dve")
                def mrhs(e, vti=vti):
                    e.matmul(pX[0:64, 0:64], Aq[:, :], S0M[:, :], start=True,
                             stop=False)
                    return e.matmul(pX[0:64, 0:64], Mks[:, :], vti,
                                    start=False, stop=True)
                m4 = p.op("pe", mrhs, waits=[e3, qtok], inc="pe")
                zcur = 0
                e4 = p.op("dve", lambda e: e.tensor_copy(Zb[0][:, :],
                                                         pX[0:64, 0:64]),
                          waits=[m4], inc="dve")
                ztok = e4
                pcur = None
                psrc, ptsrc = NTs, NTTs
                ptok = [e1, e2]
                for it in range(6):
                    def mz(e, psrc=psrc, ptsrc=ptsrc, zcur=zcur):
                        return e.matmul(pX[0:64, 0:64], ptsrc[:, :],
                                        Zb[zcur][:, :], start=True, stop=True)
                    mzt = p.op("pe", mz, waits=ptok + [ztok], inc="pe")

                    def ez(e, zcur=zcur):
                        return e.tensor_add(Zb[1 - zcur][:, :], Zb[zcur][:, :],
                                            pX[0:64, 0:64])
                    ztok = p.op("dve", ez, waits=[mzt], inc="dve")
                    zcur = 1 - zcur
                    if it < 5:
                        pb_ = it % 2

                        def msq(e, psrc=psrc, ptsrc=ptsrc):
                            e.matmul(pm[3][:, 0:64], ptsrc[:, :], psrc[:, :],
                                     start=True, stop=True)
                            return e.matmul(pm[4][:, 0:64], psrc[:, :],
                                            ptsrc[:, :], start=True, stop=True)
                        mst = p.op("pe", msq, waits=ptok + [ztok], inc="pe")

                        def esq(e, pb_=pb_):
                            e.tensor_copy(Pb[pb_][:, :], pm[3][:, 0:64])
                            return e.tensor_copy(PTb[pb_][:, :], pm[4][:, 0:64])
                        et = p.op("dve", esq, waits=[mst], inc="dve")
                        psrc, ptsrc = Pb[pb_], PTb[pb_]
                        ptok = [et]
                m5 = p.op("pe", lambda e: e.matmul(pm[1][:, 0:64], Bq[:, :],
                                                   Rq[:, :], start=True,
                                                   stop=True),
                          waits=[e2, qtok], inc="pe")
                e5 = p.op("dve", lambda e: e.tensor_mul(Sbs[:, :],
                                                        pm[1][:, 0:64],
                                                        mask_upI),
                          waits=[m5], inc="dve")
                m6 = p.op("pe", lambda e: e.matmul(pm[2][:, 0:64], Kq[:, :],
                                                   Rq[:, :], start=True,
                                                   stop=True),
                          waits=[e3, qtok], inc="pe")
                e6 = p.op("dve", lambda e: e.tensor_mul(Sks[:, :],
                                                        pm[2][:, 0:64],
                                                        mask_upI),
                          waits=[m6], inc="dve")
                def my(e, vti=vti, zcur=zcur):
                    e.matmul(pm[0][:, 0:64], S0M[:, :], Rq[:, :], start=True,
                             stop=False)
                    e.matmul(pm[0][:, 0:64], Zb[zcur][:, :], Sbs[:, :],
                             start=False, stop=False)
                    return e.matmul(pm[0][:, 0:64], vti, Sks[:, :],
                                    start=False, stop=True)
                m7 = p.op("pe", my, waits=[e1, e5, e6, ztok], inc="pe")
                e7 = p.op("dve",
                          lambda e, csl=csl: e.tensor_copy(hY[:, csl],
                                                           pm[0][:, 0:64]),
                          waits=[m7], inc="dve")
                def mtr(e):
                    return e.transpose(pTr[:, :], Bf[:, :],
                                       c["ident"][0:64, 0:64])
                t1 = p.op("pe", mtr, waits=[qtok, e7], inc="pe")
                te1 = p.op("dve", lambda e: e.tensor_copy(BfT[:, :],
                                                          pTr[:, :]),
                           waits=[t1], inc="dve")

                def mtr2(e):
                    return e.transpose(pTr[:, :], Kf[:, :],
                                       c["ident"][0:64, 0:64])
                t2 = p.op("pe", mtr2, waits=[te1], inc="pe")
                te2 = p.op("dve", lambda e: e.tensor_copy(KfT[:, :],
                                                          pTr[:, :]),
                           waits=[t2], inc="dve")
                def mst(e, vti=vti, zcur=zcur):
                    e.matmul(pm[3][:, 0:64], BfT[:, :], Zb[zcur][:, :],
                             start=True, stop=False)
                    e.matmul(pm[3][:, 0:64], KfT[:, :], vti, start=False,
                             stop=False)
                    return e.matmul(pm[3][:, 0:64], diagm[:, :], Stt[:, :],
                                    start=False, stop=True)
                m8 = p.op("pe", mst, waits=[te1, te2, ztok], inc="pe")
                ydone = p.op("dve", lambda e: e.tensor_copy(Stt[:, :],
                                                            pm[3][:, 0:64]),
                             waits=[m8], inc="dve")
            p.dma("sp", sl["y"][cc, rb:rb + 64, t0:t0 + S], hY[:, :],
                  waits=[p.now("dve")])
        scandone = [p.now("pe"), p.now("dve"), p.now("act"), p.now("d_sp")]

        for cc in range(NCHUNK):
            p.dma("sp", xdb[:, cc, :], sl["y"][cc, :, t0:t0 + S],
                  waits=scandone if cc == 0 else [])
        yload = p.now("d_sp")
        eps_gn = 1e-5 * H
        bap_g, btk_g = c.bias(eps_gn, 2)
        for cc in range(NCHUNK):
            for tt in range(NT):
                tsl = slice(tt * 512, (tt + 1) * 512)
                sqtok = p.op("act",
                             lambda e, cc=cc, tsl=tsl: e.activation(
                                 sqt[:, :], xdb[:, cc, tsl], AF.Square),
                             waits=[yload, p.now("pe"), p.now("dve")],
                             inc="act")

                def mmgn(e, cc=cc, tsl=tsl):
                    e.matmul(pm[0][0:2, :], hind[:, :], xdb[:, cc, tsl],
                             start=True, stop=True)
                    return e.matmul(pm[1][0:2, :], hind[:, :], sqt[:, :],
                                    start=True, stop=True)
                stok = p.op("pe", mmgn, waits=[sqtok], inc="pe")

                def gnmath(e):
                    e.tensor_scalar(g32[:, 0:512], pm[0][0:2, :], 1.0 / 64,
                                    None, ALU.mult)
                    e.tensor_scalar(g32[:, 512:1024], pm[1][0:2, :], 1.0 / 64,
                                    None, ALU.mult)
                    e.tensor_mul(g32[:, 1024:1536], g32[:, 0:512],
                                 g32[:, 0:512])
                    return e.tensor_sub(g32[:, 512:1024], g32[:, 512:1024],
                                        g32[:, 1024:1536])
                gt = p.op("dve", gnmath, waits=[stok], inc="dve")
                lt = p.op("act",
                          lambda e, bap_g=bap_g: e.activation(
                              g32[:, 1024:1536], g32[:, 512:1024], AF.Ln,
                              bias=bap_g),
                          waits=[gt, btk_g], inc="act")

                def gnfin(e):
                    e.activation(g2row[:, 512:1024], g32[:, 1024:1536], AF.Exp,
                                 scale=-0.5)
                    return e.mul(g2row[:, 0:512], g32[:, 0:512], -1.0)
                ft = p.op("act", gnfin, waits=[lt], inc="act")

                def gbc(e):
                    e.matmul(pX[:, :], hindT[:, :], g2row[:, 0:512],
                             start=True, stop=True)
                    return e.matmul(pY[:, :], hindT[:, :], g2row[:, 512:1024],
                                    start=True, stop=True)
                bt = p.op("pe", gbc, waits=[ft, p.now("dve")], inc="pe")
                l1 = p.dma("sp", stg[:, 0, :], sl["r"][cc, :, t0 + tt * 512:
                                                       t0 + (tt + 1) * 512],
                           waits=[p.now("dve")])
                l2 = p.dma("sp", stg[:, 1, :], sl["kn"][cc, :, t0 + tt * 512:
                                                        t0 + (tt + 1) * 512])
                l3 = p.dma("sp", stg[:, 2, :], sl["vfm"][cc, :, t0 + tt * 512:
                                                         t0 + (tt + 1) * 512])
                l4 = p.dma("sp", stg[:, 3, :], sl["gg"][cc, :, t0 + tt * 512:
                                                        t0 + (tt + 1) * 512])
                def zfin1(e, cc=cc, tsl=tsl):
                    ysl = xdb[:, cc, tsl]
                    e.tensor_add(x1t[:, 0, :], ysl, pX[:, :])
                    e.tensor_mul(x1t[:, 0, :], x1t[:, 0, :], pY[:, :])
                    return e.tensor_scalar(x1t[:, 0, :], x1t[:, 0, :],
                                           cols[:, cc, GNG:GNG + 1],
                                           cols[:, cc, GNB:GNB + 1],
                                           ALU.mult, ALU.add)
                z1t = p.op("dve", zfin1, waits=[bt, l1, l2, l3, l4], inc="dve")
                def brow(e, cc=cc):
                    return e.scalar_tensor_tensor(
                        sqt[:, :], stg[:, 0, :], cols[:, cc, XTIME:XTIME + 1],
                        stg[:, 1, :], ALU.mult, ALU.mult)
                br = p.op("dve", brow, waits=[l1, l2, z1t], inc="dve")
                mb1 = p.op("pe",
                           lambda e: e.matmul(pm[0][0:2, :], hind[:, :],
                                              sqt[:, :], start=True,
                                              stop=True),
                           waits=[br], inc="pe")
                cp1 = p.op("act",
                           lambda e: e.copy(g2row[:, 0:512], pm[0][0:2, :]),
                           waits=[mb1], inc="act")
                mb2 = p.op("pe",
                           lambda e: e.matmul(pX[:, :], hindT[:, :],
                                              g2row[:, 0:512],
                                              start=True, stop=True),
                           waits=[cp1, z1t], inc="pe")

                def zfin2(e, cc=cc, tsl=tsl):
                    e.tensor_mul(x1t[:, 1, :], stg[:, 2, :], pX[:, :])
                    e.tensor_add(x1t[:, 0, :], x1t[:, 0, :], x1t[:, 1, :])
                    return e.tensor_mul(knb[:, cc, tsl], x1t[:, 0, :],
                                        stg[:, 3, :])
                p.op("dve", zfin2, waits=[mb2], inc="dve")
        zdone = [p.now("pe"), p.now("dve"), p.now("act")]
        for half in range(2):
            wt = load_wslab("l_oW", half, guard=zdone)
            for tt in range(NT):
                for cl in range(4):
                    co = half * 4 + cl

                    def mmo(e, tt=tt, cl=cl):
                        for cin in range(NCHUNK):
                            i = e.matmul(pX[:, :],
                                         wsb[:, cin, cl * 128:cl * 128 + 128],
                                         knb[:, cin, tt * 512:(tt + 1) * 512],
                                         start=(cin == 0),
                                         stop=(cin == NCHUNK - 1))
                        return i
                    mtok = p.op("pe", mmo, waits=zdone + [wt, p.now("dve")],
                                inc="pe")
                    p.op("dve",
                         lambda e, tt=tt, co=co: e.tensor_add(
                             xdb[:, co, tt * 512:(tt + 1) * 512],
                             xin[:, co, tt * 512:(tt + 1) * 512], pX[:, :]),
                         waits=[mtok], inc="dve")
        prev = [p.now("pe"), p.now("dve"), p.now("act")]
        stok = None
        dmatok = [p.now("d_sp"), p.now("d_sp")]
        bap_e, btk_e = c.bias(1e-5, 1)
        for tt in range(NT):
            for co in range(NCHUNK):
                sqtok = p.op("act",
                             lambda e, tt=tt, co=co: e.activation(
                                 sqt[:, :], xdb[:, co, tt * 512:(tt + 1) * 512],
                                 AF.Square),
                             waits=prev + ([stok] if stok else []), inc="act")

                def mmstat(e, tt=tt, co=co):
                    e.matmul(pm[0][0:1, :], c["ones_col"][:, :],
                             xdb[:, co, tt * 512:(tt + 1) * 512],
                             start=(co == 0), stop=(co == NCHUNK - 1))
                    return e.matmul(pm[1][0:1, :], c["ones_col"][:, :],
                                    sqt[:, :], start=(co == 0),
                                    stop=(co == NCHUNK - 1))
                stok = p.op("pe", mmstat, waits=[sqtok], inc="pe")

            def rowmath(e):
                e.tensor_scalar(srow[0:1, 0:512], pm[0][0:1, :], 1.0 / D, None,
                                ALU.mult)
                e.tensor_scalar(srow[0:1, 512:1024], pm[1][0:1, :], 1.0 / D,
                                None, ALU.mult)
                e.tensor_mul(srow[0:1, 1024:1536], srow[0:1, 0:512], srow[0:1, 0:512])
                return e.tensor_sub(srow[0:1, 512:1024], srow[0:1, 512:1024],
                                    srow[0:1, 1024:1536])
            rtok = p.op("dve", rowmath, waits=[stok], inc="dve")
            ltok = p.op("act",
                        lambda e, bap_e=bap_e: e.activation(
                            srow[0:1, 1024:1536], srow[0:1, 512:1024], AF.Ln,
                            bias=bap_e),
                        waits=[rtok, btk_e], inc="act")

            def rowfin(e):
                e.mul(mrow[0:1, 0:512], srow[0:1, 0:512], -1.0)
                return e.activation(mrow[0:1, 512:1024], srow[0:1, 1024:1536],
                                    AF.Exp, scale=-0.5)
            ftok = p.op("act", rowfin, waits=[ltok], inc="act")
            stok = rtok

            def bcast(e):
                e.matmul(pX[:, :], c["ones_row"][:, :], mrow[0:1, 0:512],
                         start=True, stop=True)
                return e.matmul(pY[:, :], c["ones_row"][:, :],
                                mrow[0:1, 512:1024], start=True, stop=True)
            btok = p.op("pe", bcast, waits=[ftok, p.now("dve")], inc="pe")
            for co in range(NCHUNK):
                xb = co % 2
                w = [btok] + ([dmatok[xb]] if dmatok[xb] else [])

                def apply1(e, tt=tt, co=co, xb=xb):
                    ysl = xdb[:, co, tt * 512:(tt + 1) * 512]
                    e.tensor_add(x1t[:, xb, :], ysl, pX[:, :])
                    return e.tensor_mul(x1t[:, xb, :], x1t[:, xb, :], pY[:, :])
                atok = p.op("dve", apply1, waits=w, inc="dve")
                dmatok[xb] = p.dma(
                    "sp", xout_d[co, :, t0 + tt * 512:t0 + (tt + 1) * 512],
                    x1t[:, xb, :], waits=[atok])
    return


def prep_l1_attn_weights(inp):
    def T(a):
        return np.ascontiguousarray(np.asarray(a, np.float32).T)

    def slab(a):
        a = np.asarray(a, np.float32)
        return np.ascontiguousarray(a.reshape(8, 128, a.shape[1]))

    C = 1024
    cols = np.zeros((1024, 12), np.float32)
    for g in range(6):
        cols[:, g] = np.asarray(inp["L1_attn_lambdas"], np.float32)[g, 0, 0]
    cols[:, 6] = np.asarray(inp["L1_attn_x_miu"], np.float32).reshape(-1)
    cols[:, 7] = np.asarray(inp["L1_attn_w_miu"], np.float32).reshape(-1)
    cols[:, 8] = np.asarray(inp["L1_attn_a_time"], np.float32).reshape(-1)
    cols[:, 9] = np.asarray(inp["L1_attn_gn_gamma"], np.float32)
    cols[:, 10] = np.asarray(inp["L1_attn_gn_beta"], np.float32)
    cols[:, 11] = np.asarray(inp["L1_attn_x_time"], np.float32).reshape(-1)
    L = 64
    masks = np.concatenate([
        np.tril(np.ones((L, L), np.float32), -1),
        np.triu(np.ones((L, L), np.float32), 1),
        np.triu(np.ones((L, L), np.float32), 0)], axis=1)
    hind = np.zeros((128, 2), np.float32)
    hind[0:64, 0] = 1.0
    hind[64:128, 1] = 1.0
    xA = np.asarray(inp["L1_attn_x_A"], np.float32)
    out = {
        "l_rW": slab(T(inp["L1_attn_rW"])),
        "l_kW": slab(T(inp["L1_attn_kW"])),
        "l_vW": slab(T(inp["L1_attn_vW"])),
        "l_oW": slab(T(inp["L1_attn_oW"])),
        "l_xA0": slab(np.ascontiguousarray(xA[:, 0:96])),
        "l_xA1": slab(np.ascontiguousarray(xA[:, 96:192])),
        "l_xB": np.ascontiguousarray(np.asarray(inp["L1_attn_x_B"], np.float32)),
        "l_kA": slab(np.asarray(inp["L1_attn_k_A"], np.float32)),
        "l_kB": np.ascontiguousarray(np.asarray(inp["L1_attn_k_B"], np.float32)),
        "l_wA": slab(np.asarray(inp["L1_attn_w_A"], np.float32)),
        "l_wB": np.ascontiguousarray(np.asarray(inp["L1_attn_w_B"], np.float32)),
        "l_aA": slab(np.asarray(inp["L1_attn_a_A"], np.float32)),
        "l_aB": np.ascontiguousarray(np.asarray(inp["L1_attn_a_B"], np.float32)),
        "l_gA": slab(np.asarray(inp["L1_attn_g_A"], np.float32)),
        "l_gB": np.ascontiguousarray(np.asarray(inp["L1_attn_g_B"], np.float32)),
        "l_cols": np.ascontiguousarray(cols.reshape(8, 128, 12)),
        "l_masks": masks,
        "l_hind": hind,
        "l_hindT": np.ascontiguousarray(hind.T),
    }
    return out


def declare_l1_slabs(nc, S):
    sl = {}
    for nm in ("r", "kk", "bb", "kn", "vfm", "gg", "y"):
        sl[nm] = nc.dram_tensor("sl_" + nm, [NCHUNK, 128, 2 * S], BF16,
                                kind="Internal").ap()
    sl["w"] = nc.dram_tensor("sl_w", [NCHUNK, 128, 2 * S], F32,
                             kind="Internal").ap()
    sl["vtm"] = nc.dram_tensor("sl_vtm", [2 * S // 128, 128, 1024], BF16,
                               kind="Internal").ap()
    sl["mix"] = nc.dram_tensor("sl_mix", [192, 2 * S], BF16,
                               kind="Internal").ap()
    return sl


def run_l1_attn(x_full, inp, S=2048):
    ncores = 8
    host_w = prep_l1_attn_weights(inp)
    nc = bass.Bass("TRN2", target_bir_lowering=False)
    xin_d = nc.dram_tensor("x_in", [NCHUNK, 128, 2 * S], F32,
                           kind="ExternalInput").ap()
    xout_d = nc.dram_tensor("x_out", [NCHUNK, 128, 2 * S], F32,
                            kind="ExternalOutput").ap()
    sl = declare_l1_slabs(nc, S)
    p = Prog(nc)
    host_c = {}
    with ExitStack() as es:
        wd = declare_weights(nc, host_w)
        cst = alloc_consts(nc, es, p, host_c)
        build_l1_attn(nc, es, p, cst, xin_d, xout_d, sl, wd, S)
        p.op("sp", lambda e: e.nop(), waits=p.all_now())
        p.build(es)
    in_maps = []
    for i in range(ncores):
        xi = np.concatenate([x_full[2 * i], x_full[2 * i + 1]], 0)
        m = {"x_in": fm_slab(xi)}
        m.update(host_w)
        m.update(host_c)
        in_maps.append(m)
    res = run_bass_kernel_spmd(nc, in_maps, core_ids=list(range(ncores)))
    outs = []
    for i in range(ncores):
        o = from_fm_slab(res.results[i]["x_out"])
        outs.append(o[:S])
        outs.append(o[S:])
    return np.stack(outs)



def build_l1_ffn(nc, es, p, c, xin_d, xout_d, wd, S, dbg=None):
    NB = 2
    E = 8
    NT = S // 512
    NTI = S // 128
    F = 4096

    def sbuf(name, shape, dtype):
        return es.enter_context(nc.sbuf_tensor("mb_" + name, shape, dtype))

    def psum(name, shape, dtype=F32):
        return es.enter_context(nc.psum_tensor("mp_" + name, shape, dtype))

    acc = sbuf("acc", [128, NCHUNK, S], F32)
    xb = sbuf("xb", [128, NCHUNK, S], BF16)
    wfc = sbuf("wfc", [128, NCHUNK, 2048], BF16)
    wpj = sbuf("wpj", [128, 16, 1024], BF16)
    hbuf = sbuf("hbuf", [128, 16, 512], BF16)
    combT = sbuf("combT", [8, S], BF16)
    centT = sbuf("centT", [128, NCHUNK, 8], F32)
    xrt = sbuf("xrt", [128, NCHUNK, 128], F32)
    bias_bc = sbuf("bias_bc", [128, 8], F32)
    selbc = sbuf("selbc", [8, E * 128], BF16)
    rs = sbuf("rs", [128, 8], F32)
    bal = sbuf("bal", [128, 8], F32)
    msk = sbuf("msk", [128, 6], F32)
    mk1 = sbuf("mk1", [128, 8], F32)
    mk2 = sbuf("mk2", [128, 8], F32)
    cmb16 = sbuf("cmb16", [128, 8], F32)
    cbc = sbuf("cbc", [128, 512], BF16)
    srow = sbuf("srow", [1, 1536], F32)
    mrow = sbuf("mrow", [1, 1024], BF16)
    sqt = sbuf("sqt", [128, 512], BF16)
    x1t = sbuf("x1t", [128, 2, 512], F32)
    xst = sbuf("xst", [128, 2, 512], F32)

    pG = [psum("pG%d" % i, [128, 512]) for i in range(2)]
    pP = psum("pP", [128, 512])
    pB = psum("pB", [128, 512])
    pS = [psum("pS%d" % i, [1, 512]) for i in range(2)]
    pBC = [psum("pBC%d" % i, [128, 512]) for i in range(2)]

    for cc in range(NCHUNK):
        p.dma("sp", centT[:, cc, :], wd["m_cent"][cc, :, :])
    p.dma("sp", bias_bc[:, :], wd["m_biasbc"])
    p.dma("pool", selbc[:, :], wd["m_selbc"])
    cload = [p.now("d_pool"), p.now("d_sp")]

    for b in range(NB):
        t0 = b * S
        xw = [p.now("pe"), p.now("dve"), p.now("act"),
              p.now("d_sp")] if b > 0 else []
        for cc in range(NCHUNK):
            p.dma("pool", xb[:, cc, :], xin_d[cc, :, t0:t0 + S], waits=xw)
        xload = [p.now("d_pool")] + cload
        dmt = [p.now("d_sp"), p.now("d_sp")]
        for cc in range(NCHUNK):
            for tt in range(NT):
                sb = (cc * NT + tt) % 2
                w = ([dmt[sb]] if dmt[sb] else []) + (xw if cc == 0 else [])
                ldt = p.dma("sp", xst[:, sb, :],
                            xin_d[cc, :, t0 + tt * 512:t0 + (tt + 1) * 512],
                            waits=w)
                dmt[sb] = p.op(
                    "dve",
                    lambda e, cc=cc, tt=tt, sb=sb: e.tensor_scalar(
                        acc[:, cc, tt * 512:(tt + 1) * 512], xst[:, sb, :],
                        2.0, None, ALU.mult),
                    waits=[ldt], inc="dve")
        accinit = [p.now("dve"), p.now("d_sp")]

        for ti in range(NTI):
            xrl = p.dma("sp", xrt[:, :, :],
                        xin_d[:, :, t0 + ti * 128:t0 + (ti + 1) * 128]
                        .rearrange("c p t -> p c t"),
                        waits=[p.now("pe")])

            def mmr(e, ti=ti):
                for cin in range(NCHUNK):
                    i = e.matmul(pP[:, 0:8], xrt[:, cin, :],
                                 centT[:, cin, :], start=(cin == 0),
                                 stop=(cin == NCHUNK - 1))
                return i
            mtok = p.op("pe", mmr,
                        waits=xload + [xrl, p.now("act"), p.now("dve")],
                        inc="pe")
            stok = p.op("act",
                        lambda e: e.activation(rs[:, :], pP[:, 0:8],
                                               AF.Sigmoid),
                        waits=[mtok], inc="act")

            def balf(e):
                return e.tensor_add(bal[:, :], bias_bc[:, :], pP[:, 0:8])
            btok = p.op("dve", balf, waits=[stok], inc="dve")

            ttok = p.op("dve", lambda e: e.reduce_max(msk[:, 0:1], bal[:, :],
                                                      axis=AX.X),
                        waits=[btok], inc="dve")
            ttok = p.op("dve", lambda e: e.tensor_scalar(
                mk1[:, :], bal[:, :], msk[:, 0:1], None, ALU.is_equal),
                waits=[ttok], inc="dve")
            ttok = p.op("dve", lambda e: e.tensor_scalar(
                mk2[:, :], mk1[:, :], -1e9, None, ALU.mult),
                waits=[ttok], inc="dve")
            ttok = p.op("dve", lambda e: e.tensor_add(mk2[:, :], mk2[:, :],
                                                      bal[:, :]),
                        waits=[ttok], inc="dve")
            ttok = p.op("dve", lambda e: e.reduce_max(msk[:, 1:2], mk2[:, :],
                                                      axis=AX.X),
                        waits=[ttok], inc="dve")
            ttok = p.op("dve", lambda e: e.tensor_scalar(
                mk2[:, :], mk2[:, :], msk[:, 1:2], None, ALU.is_equal),
                waits=[ttok], inc="dve")
            ttok = p.op("dve", lambda e: e.tensor_mul(bal[:, :], rs[:, :],
                                                      mk1[:, :]),
                        waits=[ttok], inc="dve")
            ttok = p.op("dve", lambda e: e.reduce_sum(msk[:, 2:3], bal[:, :],
                                                      axis=AX.X),
                        waits=[ttok], inc="dve")
            ttok = p.op("dve", lambda e: e.tensor_mul(bal[:, :], rs[:, :],
                                                      mk2[:, :]),
                        waits=[ttok], inc="dve")
            ttok = p.op("dve", lambda e: e.reduce_sum(msk[:, 3:4], bal[:, :],
                                                      axis=AX.X),
                        waits=[ttok], inc="dve")
            ttok = p.op("dve", lambda e: e.tensor_sub(msk[:, 4:5], msk[:, 2:3],
                                                      msk[:, 3:4]),
                        waits=[ttok], inc="dve")
            wtok = p.op("act",
                        lambda e: e.activation(msk[:, 4:5], msk[:, 4:5],
                                               AF.Sigmoid),
                        waits=[ttok], inc="act")

            ctok = p.op("dve", lambda e: e.tensor_scalar(
                msk[:, 5:6], msk[:, 4:5], -1.0, 1.0, ALU.mult, ALU.add),
                waits=[wtok], inc="dve")
            ctok = p.op("dve", lambda e: e.tensor_scalar(
                mk1[:, :], mk1[:, :], msk[:, 4:5], None, ALU.mult),
                waits=[ctok], inc="dve")
            ctok = p.op("dve", lambda e: e.tensor_scalar(
                mk2[:, :], mk2[:, :], msk[:, 5:6], None, ALU.mult),
                waits=[ctok], inc="dve")
            ctok = p.op("dve", lambda e: e.tensor_add(cmb16[:, :], mk1[:, :],
                                                      mk2[:, :]),
                        waits=[ctok], inc="dve")
            trtok = p.op("pe",
                         lambda e: e.transpose(pBC[0][0:8, 0:128], cmb16[:, :],
                                               c["identf"][:, :]),
                         waits=[ctok, p.now("dve")], inc="pe")
            p.op("dve",
                 lambda e, ti=ti: e.tensor_copy(
                     combT[:, ti * 128:(ti + 1) * 128], pBC[0][0:8, 0:128]),
                 waits=[trtok], inc="dve")
        rdone = [p.now("pe"), p.now("dve"), p.now("act")]
        if dbg is not None and b == 0:
            p.dma("pool", dbg["combT"], combT[:, :], waits=rdone)
            p.dma("pool", dbg["rs"], rs[:, :], waits=rdone)
            p.dma("pool", dbg["bal"], bal[:, :], waits=rdone)
            p.dma("pool", dbg["mk1"], mk1[:, :], waits=rdone)

        for ex in range(10):
            routed = ex < E
            fcname = ("m_fc_r%d" % ex) if routed else ("m_fc_s%d" % (ex - E))
            pjname = ("m_pj_r%d" % ex) if routed else ("m_pj_s%d" % (ex - E))
            for half in range(2):
                g = [p.now("pe")]
                for cc in range(NCHUNK):
                    p.dma("pool", wfc[:, cc, :],
                          wd[fcname][cc, :, half * 2048:(half + 1) * 2048],
                          waits=g)
                for fo in range(16):
                    p.dma("pool", wpj[:, fo, :],
                          wd[pjname][half * 16 + fo, :, :], waits=g)
                wt = p.now("d_pool")
                for tt in range(NT):
                    gtoks = [None, None]
                    for fo in range(16):
                        pg = fo % 2
                        w = [wt] + ([gtoks[pg]] if gtoks[pg] else [])

                        def mmh(e, tt=tt, fo=fo, pg=pg):
                            for cin in range(NCHUNK):
                                i = e.matmul(
                                    pG[pg][:, :],
                                    wfc[:, cin, fo * 128:(fo + 1) * 128],
                                    xb[:, cin, tt * 512:(tt + 1) * 512],
                                    start=(cin == 0), stop=(cin == NCHUNK - 1))
                            return i
                        mtok = p.op("pe", mmh, waits=w + rdone, inc="pe")
                        gtoks[pg] = p.op(
                            "act",
                            lambda e, fo=fo, pg=pg: e.activation(
                                hbuf[:, fo, :], pG[pg][:, :], AF.Gelu),
                            waits=[mtok], inc="act")
                    hdone = [p.now("act")]
                    if routed:
                        bct = p.op(
                            "pe",
                            lambda e, ex=ex, tt=tt: e.matmul(
                                pB[:, :], selbc[:, ex * 128:(ex + 1) * 128],
                                combT[:, tt * 512:(tt + 1) * 512],
                                start=True, stop=True),
                            waits=rdone + [p.now("dve")], inc="pe")
                        bce = p.op("dve",
                                   lambda e: e.tensor_copy(cbc[:, :], pB[:, :]),
                                   waits=[bct], inc="dve")
                    for co in range(NCHUNK):
                        def mmp(e, tt=tt, co=co):
                            for fo in range(16):
                                i = e.matmul(
                                    pP[:, :],
                                    wpj[:, fo, co * 128:(co + 1) * 128],
                                    hbuf[:, fo, :],
                                    start=(fo == 0), stop=(fo == 15))
                            return i
                        mtok = p.op("pe", mmp,
                                    waits=hdone + [wt, p.now("dve")], inc="pe")
                        if routed:
                            def accf(e, tt=tt, co=co):
                                e.tensor_mul(x1t[:, 0, :], cbc[:, :], pP[:, :])
                                asl = acc[:, co, tt * 512:(tt + 1) * 512]
                                return e.tensor_add(asl, asl, x1t[:, 0, :])
                            p.op("dve", accf, waits=[mtok, bce] + accinit,
                                 inc="dve")
                        else:
                            def accf(e, tt=tt, co=co):
                                asl = acc[:, co, tt * 512:(tt + 1) * 512]
                                return e.tensor_add(asl, asl, pP[:, :])
                            p.op("dve", accf, waits=[mtok] + accinit,
                                 inc="dve")
        exdone = [p.now("pe"), p.now("dve"), p.now("act")]

        for cc in range(NCHUNK):
            for tt in range(NT):
                p.op("dve",
                     lambda e, cc=cc, tt=tt: e.tensor_copy(
                         xb[:, cc, tt * 512:(tt + 1) * 512],
                         acc[:, cc, tt * 512:(tt + 1) * 512]),
                     waits=exdone, inc="dve")
        cpdone = [p.now("dve")]
        prev = exdone + cpdone
        stok = None
        dmatok = [p.now("d_sp"), p.now("d_sp")]
        bap_e, btk_e = c.bias(1e-5, 1)
        for tt in range(NT):
            for co in range(NCHUNK):
                sqtok = p.op("act",
                             lambda e, tt=tt, co=co: e.activation(
                                 sqt[:, :], xb[:, co, tt * 512:(tt + 1) * 512],
                                 AF.Square),
                             waits=prev + ([stok] if stok else []), inc="act")

                def mmstat(e, tt=tt, co=co):
                    e.matmul(pS[0][:, :], c["ones_col"][:, :],
                             xb[:, co, tt * 512:(tt + 1) * 512],
                             start=(co == 0), stop=(co == NCHUNK - 1))
                    return e.matmul(pS[1][:, :], c["ones_col"][:, :],
                                    sqt[:, :], start=(co == 0),
                                    stop=(co == NCHUNK - 1))
                stok = p.op("pe", mmstat, waits=[sqtok], inc="pe")

            def rowmath(e):
                e.tensor_scalar(srow[:, 0:512], pS[0][:, :], 1.0 / D, None,
                                ALU.mult)
                e.tensor_scalar(srow[:, 512:1024], pS[1][:, :], 1.0 / D,
                                None, ALU.mult)
                e.tensor_mul(srow[:, 1024:1536], srow[:, 0:512],
                             srow[:, 0:512])
                return e.tensor_sub(srow[:, 512:1024], srow[:, 512:1024],
                                    srow[:, 1024:1536])
            rtok = p.op("dve", rowmath, waits=[stok], inc="dve")
            ltok = p.op("act",
                        lambda e, bap_e=bap_e: e.activation(
                            srow[:, 1024:1536], srow[:, 512:1024], AF.Ln,
                            bias=bap_e),
                        waits=[rtok, btk_e], inc="act")

            def rowfin(e):
                e.mul(mrow[:, 0:512], srow[:, 0:512], -1.0)
                return e.activation(mrow[:, 512:1024], srow[:, 1024:1536],
                                    AF.Exp, scale=-0.5)
            ftok = p.op("act", rowfin, waits=[ltok], inc="act")
            stok = rtok

            def bcast(e):
                e.matmul(pBC[0][:, :], c["ones_row"][:, :], mrow[0:1, 0:512],
                         start=True, stop=True)
                return e.matmul(pBC[1][:, :], c["ones_row"][:, :],
                                mrow[0:1, 512:1024], start=True, stop=True)
            btok = p.op("pe", bcast, waits=[ftok, p.now("dve")], inc="pe")
            for co in range(NCHUNK):
                xp = co % 2
                w = [btok] + ([dmatok[xp]] if dmatok[xp] else [])

                def apply1(e, tt=tt, co=co, xp=xp):
                    asl = acc[:, co, tt * 512:(tt + 1) * 512]
                    e.tensor_add(x1t[:, xp, :], asl, pBC[0][:, :])
                    return e.tensor_mul(x1t[:, xp, :], x1t[:, xp, :],
                                        pBC[1][:, :])
                atok = p.op("dve", apply1, waits=w, inc="dve")
                dmatok[xp] = p.dma(
                    "sp", xout_d[co, :, t0 + tt * 512:t0 + (tt + 1) * 512],
                    x1t[:, xp, :], waits=[atok])
    return


def prep_l1_ffn_weights(inp):
    out = {}
    cent = np.asarray(inp["L1_ffn_centroids"], np.float32)
    out["m_cent"] = np.ascontiguousarray(cent.T.reshape(8, 128, 8))
    out["m_biasbc"] = np.ascontiguousarray(
        np.tile(np.asarray(inp["L1_ffn_routing_bias"], np.float32)
                .reshape(1, 8), (128, 1)))
    sel = np.zeros((8, 8 * 128), np.float32)
    for e in range(8):
        sel[e, e * 128:(e + 1) * 128] = 1.0
    out["m_selbc"] = sel
    Wfc_r = np.asarray(inp["L1_ffn_Wfc_r"], np.float32)
    Wpj_r = np.asarray(inp["L1_ffn_Wproj_r"], np.float32)
    Wfc_s = np.asarray(inp["L1_ffn_Wfc_sh"], np.float32)
    Wpj_s = np.asarray(inp["L1_ffn_Wproj_sh"], np.float32)
    for e in range(8):
        out["m_fc_r%d" % e] = np.ascontiguousarray(
            Wfc_r[e].T.reshape(8, 128, 4096))
        out["m_pj_r%d" % e] = np.ascontiguousarray(
            Wpj_r[e].T.reshape(32, 128, 1024))
    for n in range(2):
        out["m_fc_s%d" % n] = np.ascontiguousarray(
            Wfc_s[n].T.reshape(8, 128, 4096))
        out["m_pj_s%d" % n] = np.ascontiguousarray(
            Wpj_s[n].T.reshape(32, 128, 1024))
    return out


def run_l1_ffn(x_full, inp, S=2048, debug=False):
    ncores = 8
    host_w = prep_l1_ffn_weights(inp)
    nc = bass.Bass("TRN2", target_bir_lowering=False)
    xin_d = nc.dram_tensor("x_in", [NCHUNK, 128, 2 * S], F32,
                           kind="ExternalInput").ap()
    xout_d = nc.dram_tensor("x_out", [NCHUNK, 128, 2 * S], F32,
                            kind="ExternalOutput").ap()
    dbg = None
    if debug:
        dbg = {
            "combT": nc.dram_tensor("dbg_combT", [8, S], F32, kind="ExternalOutput").ap(),
            "rs": nc.dram_tensor("dbg_rs", [128, 8], F32, kind="ExternalOutput").ap(),
            "bal": nc.dram_tensor("dbg_bal", [128, 8], F32, kind="ExternalOutput").ap(),
            "mk1": nc.dram_tensor("dbg_mk1", [128, 8], F32, kind="ExternalOutput").ap(),
        }
    p = Prog(nc)
    host_c = {}
    with ExitStack() as es:
        wd = declare_weights(nc, host_w)
        cst = alloc_consts(nc, es, p, host_c, need_identf=True)
        build_l1_ffn(nc, es, p, cst, xin_d, xout_d, wd, S, dbg=dbg)
        p.op("sp", lambda e: e.nop(), waits=p.all_now())
        p.build(es)
    in_maps = []
    for i in range(ncores):
        xi = np.concatenate([x_full[2 * i], x_full[2 * i + 1]], 0)
        m = {"x_in": fm_slab(xi)}
        m.update(host_w)
        m.update(host_c)
        in_maps.append(m)
    res = run_bass_kernel_spmd(nc, in_maps, core_ids=list(range(ncores)))
    outs = []
    for i in range(ncores):
        o = from_fm_slab(res.results[i]["x_out"])
        outs.append(o[:S])
        outs.append(o[S:])
    if debug:
        return np.stack(outs), res.results
    return np.stack(outs)


def kernel(**inputs):
    x = np.asarray(inputs["L0_attn_x"], np.float32)
    x1 = run_l0_attn(x, inputs)
    x2 = run_l0_ffn(x1, inputs)
    x3 = run_l1_attn(x2, inputs)
    x4 = run_l1_ffn(x3, inputs)
    return x4
